# version 04
# speedup vs baseline: 1.1528x; 1.1302x over previous
import contextlib
import numpy as np
import ml_dtypes
import concourse.bass as bass
import concourse.mybir as mybir
from concourse.bass import IndirectOffsetOnAxis
from concourse.bass_utils import run_bass_kernel_spmd

F32 = mybir.dt.float32
BF16 = mybir.dt.bfloat16
I32 = mybir.dt.int32
AF = mybir.ActivationFunctionType
ALU = mybir.AluOpType
AX = mybir.AxisListType

D = 1024
T = 4096
NC = 8
NTG = 8
TG = 512
NEG = -1.0e30
LNEPS = 1e-5
NCORE = 8
DEBUG_L1F = False
L1F_SKIP_LORA = False
L2F_STAGES = 4
DEBUG_MOE = False
DEBUG_L2F = False
L2F_RSUB = 9
L2F_RITER = 50
L2F_RBODY = 5

QUEUES = {"pe": "tensor", "ve": "vector", "sc": "scalar",
          "gp": "gpsimd", "sw": "gpsimd", "hw": "sync"}


class Sem:
    def __init__(self, h):
        self.h = h
        self.n = 0


class QP:

    def __init__(self, ph, qname):
        self.ph = ph
        self.q = qname

    @property
    def n(self):
        return self.ph.cnt[self.q]

    def _live(self):
        return self.ph.live == QUEUES[self.q]

    def op(self, name, *a, tick=1, **kw):
        self.ph.cnt[self.q] += tick
        if self._live():
            try:
                ins = getattr(self.ph.eng, name)(*a, **kw)
            except Exception:
                import sys
                print(f"QP.op FAIL q={self.q} op={name} cnt={self.ph.cnt[self.q]}",
                      file=sys.stderr, flush=True)
                raise
            if tick:
                ins.then_inc(self.ph.sems[self.q].h, tick)
        return self.ph.cnt[self.q]

    def raw(self, name, *a, **kw):
        if self._live():
            getattr(self.ph.eng, name)(*a, **kw)

    def dma(self, out, in_, **kw):
        return self.op("dma_start", tick=16, out=out, in_=in_, **kw)

    def reg(self, val):
        key = (self.q, val)
        if key not in self.ph.regcache:
            self.ph.regcache[key] = self.ph.eng.to_reg(val) if self._live() else None
        return self.ph.regcache[key]

    def wait(self, other, n=None):
        if n is None:
            n = self.ph.cnt[other.q if isinstance(other, QP) else other]
        if n > 0 and self._live():
            self.ph.eng.wait_ge(self.ph.sems[other.q if isinstance(other, QP) else other].h, n)


class Phase:
    def __init__(self, ctx, name, body, engines=None):
        self.ctx = ctx
        self.sems = ctx.sems
        self.live = None
        self.eng = None
        self.base = {q: ctx.sems[q].n for q in QUEUES}
        self.cnt = dict(self.base)
        for q in QUEUES:
            setattr(self, q, QP(self, q))
        self.live = None
        self.regcache = {}
        body(self)
        final = dict(self.cnt)
        used = {q for q in QUEUES if final[q] != self.base[q]}
        if engines is None:
            engines = sorted({QUEUES[q] for q in used})
        with ctx.nc.named_scope(name):
            with ctx.nc.Block(name) as blk:
                for engname in engines:
                    def run(e, engname=engname):
                        self.cnt = dict(self.base)
                        self.live = engname
                        self.eng = e
                        self.regcache = {}
                        body(self)
                    getattr(blk, engname)(run)
        for q in QUEUES:
            ctx.sems[q].n = final[q]


class Ctx:
    def __init__(self, nc, es):
        self.nc = nc
        self.es = es
        self.t = {}
        self.d = {}
        self.sems = {}
        for nm in QUEUES:
            self.sems[nm] = Sem(es.enter_context(nc.semaphore("s_" + nm)))

    def phase(self, name, body, engines=None):
        return Phase(self, name, body, engines)

    def sb(self, name, shape, dtype=F32, stack=None):
        t = (stack or self.es).enter_context(self.nc.sbuf_tensor(name, shape, dtype))
        self.t[name] = t
        return t

    def ps(self, name, shape, dtype=F32, stack=None):
        t = (stack or self.es).enter_context(self.nc.psum_tensor(name, shape, dtype))
        self.t[name] = t
        return t

    def dram_in(self, name, shape, dtype=F32):
        ap = self.nc.dram_tensor(name, list(shape), dtype, kind="ExternalInput").ap()
        self.d[name] = ap
        return ap

    def dram_out(self, name, shape, dtype=F32):
        ap = self.nc.dram_tensor(name, list(shape), dtype, kind="ExternalOutput").ap()
        self.d[name] = ap
        return ap

    def dram_tmp(self, name, shape, dtype=F32):
        ap = self.nc.dram_tensor(name, list(shape), dtype, kind="Internal").ap()
        self.d[name] = ap
        return ap



def host_consts():
    c = {}
    c["IDENT"] = np.eye(128, dtype=np.float32).astype(ml_dtypes.bfloat16)
    c["IDENTF"] = np.eye(128, dtype=np.float32)
    c["ONES"] = np.ones((128, 128), dtype=np.float32).astype(ml_dtypes.bfloat16)
    c["ONESF"] = np.ones((128, 128), dtype=np.float32)
    sw = np.zeros((128, 128), np.float32)
    for p in range(128):
        sw[p ^ 1, p] = 1.0
    c["PSWAP"] = sw.astype(ml_dtypes.bfloat16)
    j = (np.arange(128) % 64) // 2
    inv_freq = (10000.0 ** (-2.0 * np.arange(32, dtype=np.float32) / 64)).astype(np.float32)
    ang = np.arange(T, dtype=np.float32)[None, :] * inv_freq[j][:, None]
    sgn = np.where(np.arange(128) % 2 == 0, -1.0, 1.0).astype(np.float32)[:, None]
    c["RCOS"] = np.cos(ang).astype(ml_dtypes.bfloat16)
    c["RSIN"] = (np.sin(ang) * sgn).astype(ml_dtypes.bfloat16)
    kk = np.arange(128)[:, None]
    qq = np.arange(512)[None, :]
    c["CAUSAL"] = np.stack([np.where(kk + r <= qq, 0.0, NEG).astype(np.float32)
                            for r in (0, 128, 256, 384)]).astype(ml_dtypes.bfloat16)
    offs = list(range(-384, 513, 128))
    c["WINM"] = np.stack([np.where((qq - kk + off >= 0) & (qq - kk + off <= 256), 0.0, NEG).astype(np.float32)
                          for off in offs]).astype(ml_dtypes.bfloat16)
    c["LTRI"] = np.tril(np.ones((128, 128), np.float32), -1).astype(ml_dtypes.bfloat16)
    p2 = np.zeros((128, 2), np.float32)
    p2[0:64, 0] = 1.0
    p2[64:128, 1] = 1.0
    c["PAT2F"] = p2
    c["PAT2TF"] = np.ascontiguousarray(p2.T)
    c["LTRIU"] = np.triu(np.ones((128, 128), np.float32), 1).astype(ml_dtypes.bfloat16)
    c["TOKID"] = (np.arange(32)[None, :] * 128 + np.arange(128)[:, None]).astype(np.float32)
    tok16 = np.repeat((np.arange(32)[None, :] * 128 + np.arange(128)[:, None])[:, :, None], 16, axis=2)
    c["TOKID16"] = tok16.reshape(128, 32 * 16).astype(np.float32)
    sel8 = np.zeros((8, 8 * 128), np.float32)
    for e in range(8):
        sel8[e, e * 128:(e + 1) * 128] = 1.0
    c["SEL8"] = sel8
    return c


CONSTS = host_consts()
WIN_OFFS = list(range(-384, 513, 128))


def declare_consts(ctx, names, suffix="", stack=None):
    pairs = []
    for n in names:
        v = CONSTS[n]
        dt = BF16 if v.dtype == ml_dtypes.bfloat16 else F32
        shape = list(v.shape)
        if ("c_" + n) in ctx.d:
            ap = ctx.d["c_" + n]
        else:
            ap = ctx.dram_in("c_" + n, shape, dt)
        if len(shape) == 3:
            t = ctx.sb("s_" + n + suffix, [shape[1], shape[0] * shape[2]], dt, stack=stack)
            pairs.append((t[:].rearrange("p (n f) -> p n f", n=shape[0]),
                          ap.rearrange("n p f -> p n f")))
        else:
            t = ctx.sb("s_" + n + suffix, shape, dt, stack=stack)
            pairs.append((t[:], ap))
        ctx.t["s_" + n] = t

    def body(ph):
        for dst, src in pairs:
            ph.hw.dma(dst, src)
        ph.hw.wait(ph.hw)
    ctx.phase("consts" + suffix, body)


def cv(ctx, name, i=None):
    t = ctx.t["s_" + name]
    if i is None:
        return t[:]
    n = CONSTS[name].shape[0]
    return t[:].rearrange("p (n f) -> p n f", n=n)[:, i, :]



def stream_load(ctx, src_ap, cast):
    X = [ctx.t[f"X{c}"] for c in range(NC)]

    def body(ph):
        q = ph.sw if cast else ph.hw
        for c in range(NC):
            q.dma(X[c][:], src_ap[c * 128:(c + 1) * 128, :])
        q.wait(q)
    ctx.phase("sload", body)


def stream_store(ctx, dst_ap):
    X = [ctx.t[f"X{c}"] for c in range(NC)]

    def body(ph):
        for c in range(NC):
            ph.hw.dma(dst_ap[c * 128:(c + 1) * 128, :], X[c][:])
        ph.hw.wait(ph.hw)
    ctx.phase("sstore", body)


def load_w(ctx, ph, tile_view, dram_ap):
    ph.sw.dma(tile_view, dram_ap.rearrange("(n p) f -> p n f", p=128))




def ln_finish(ctx, name, produce, out_dram=None, prodsems=("pe", "ve"), tgs=None):
    X = [ctx.t[f"X{c}"] for c in range(NC)]
    es = contextlib.ExitStack()
    with es:
        vsl = [ctx.sb(f"{name}_vs{c}", [128, TG], F32, stack=es) for c in range(NC)]
        sqt = [ctx.sb(f"{name}_q{i}", [128, TG], F32, stack=es) for i in range(2)]
        mrow = ctx.sb(f"{name}_m0", [1, TG], F32, stack=es)
        vrow = ctx.sb(f"{name}_vw", [1, TG], F32, stack=es)
        srow = ctx.sb(f"{name}_s0", [1, TG], F32, stack=es)
        rrow = ctx.sb(f"{name}_r0", [1, TG], F32, stack=es)
        mrr = ctx.sb(f"{name}_w0", [1, TG], F32, stack=es)
        tmp = ctx.sb(f"{name}_t0", [128, TG], F32, stack=es)
        ps_sum = ctx.ps(f"{name}_p0", [1, TG], F32, stack=es)
        ps_sq = ctx.ps(f"{name}_p1", [1, TG], F32, stack=es)
        ps_r = ctx.ps(f"{name}_p2", [128, TG], F32, stack=es)
        ps_m = ctx.ps(f"{name}_p3", [128, TG], F32, stack=es)
        outsl = [ctx.sb(f"{name}_o{c}", [128, TG], F32, stack=es) for c in range(NC)] \
            if out_dram is not None else None

        def body(ph):
            ones = ctx.t["s_ONESF"]
            marks = {}
            sq_done = {}
            summ_done = {}
            sqm_done = {}
            var_done = {}
            sqrt_done = {}
            stat_done = {}
            bc_done = {}
            app_done = {}
            store_done = {}

            tglist = list(range(NTG)) if tgs is None else list(tgs)
            for ti, tg in enumerate(tglist):
                ptg = tglist[ti - 1] if ti > 0 else None
                prev = None if ti == 0 else dict(
                    app={c: app_done[(ptg, c)] for c in range(NC)},
                    summ=summ_done[ptg],
                    sq={c: sq_done[(ptg, c)] for c in range(NC)})
                produce(ph, tg, vsl, marks, prev)
                pe_ap = ph.pe.n
                for j in range(NC):
                    sqm_done[(tg, j)] = pe_ap + 8 + j + 1
                for c in range(NC):
                    ph.sc.wait(*marks[(tg, c)])
                    if not (ti == 0 and c < 2):
                        key = (tg, c - 2) if c >= 2 else (ptg, NC + c - 2)
                        ph.sc.wait(ph.pe, sqm_done[key])
                    sq_done[(tg, c)] = ph.sc.op("activation", sqt[c % 2][:], vsl[c][:], AF.Square)
                if ti > 0:
                    ph.pe.wait(ph.ve, var_done[ptg])
                for c in range(NC):
                    ph.pe.wait(*marks[(tg, c)])
                    summ_done[tg] = ph.pe.op("matmul", ps_sum[:, :], ones[:, 0:1], vsl[c][:],
                                             start=(c == 0), stop=(c == NC - 1))
                for c in range(NC):
                    ph.pe.wait(ph.sc, sq_done[(tg, c)])
                    n = ph.pe.op("matmul", ps_sq[:, :], ones[:, 0:1], sqt[c % 2][:],
                                 start=(c == 0), stop=(c == NC - 1))
                    assert n == sqm_done[(tg, c)], (tg, c, n, sqm_done[(tg, c)])
                ph.ve.wait(ph.pe, sqm_done[(tg, NC - 1)])
                if ti > 0:
                    ph.ve.wait(ph.ve, app_done[(ptg, NC - 1)])
                    ph.ve.wait(ph.sc, sqrt_done[ptg])
                ph.ve.op("tensor_scalar_mul", mrow[:], ps_sum[:, :], 1.0 / D)
                n = ph.ve.op("tensor_scalar_mul", vrow[:], ps_sq[:, :], 1.0 / D)
                ph.ve.wait(ph.ve, n)
                n = ph.ve.op("tensor_tensor", srow[:], mrow[:], mrow[:], ALU.mult)
                ph.ve.wait(ph.ve, n)
                n = ph.ve.op("tensor_tensor", vrow[:], vrow[:], srow[:], ALU.subtract)
                ph.ve.wait(ph.ve, n)
                var_done[tg] = ph.ve.op("tensor_scalar_add", vrow[:], vrow[:], LNEPS)
                ph.sc.wait(ph.ve, var_done[tg])
                sqrt_done[tg] = ph.sc.op("activation", srow[:], vrow[:], AF.Sqrt)
                ph.ve.wait(ph.sc, sqrt_done[tg])
                n = ph.ve.op("reciprocal", rrow[:], srow[:])
                ph.ve.wait(ph.ve, n)
                stat_done[tg] = ph.ve.op("tensor_tensor", mrr[:], mrow[:], rrow[:], ALU.mult)
                ph.pe.wait(ph.ve, stat_done[tg])
                if ti > 0:
                    ph.pe.wait(ph.ve, app_done[(ptg, NC - 1)])
                ph.pe.op("matmul", ps_r[:, :], ones[0:1, :], rrow[:], start=True, stop=True)
                bc_done[tg] = ph.pe.op("matmul", ps_m[:, :], ones[0:1, :], mrr[:], start=True, stop=True)
                ph.ve.wait(ph.pe, bc_done[tg])
                for c in range(NC):
                    if out_dram is not None and ti > 0:
                        ph.ve.wait(ph.hw, store_done[(ptg, c)])
                    ph.ve.wait(ph.ve)
                    n = ph.ve.op("tensor_tensor", tmp[:], vsl[c][:], ps_r[:, :], ALU.mult)
                    ph.ve.wait(ph.ve, n)
                    dst = X[c][:, tg * TG:(tg + 1) * TG] if out_dram is None else outsl[c][:]
                    app_done[(tg, c)] = ph.ve.op("tensor_tensor", dst, tmp[:], ps_m[:, :], ALU.subtract)
                if out_dram is not None:
                    for c in range(NC):
                        ph.hw.wait(ph.ve, app_done[(tg, c)])
                        store_done[(tg, c)] = ph.hw.dma(
                            out_dram[c * 128:(c + 1) * 128, tg * TG:(tg + 1) * TG], outsl[c][:])
            if out_dram is not None:
                ph.hw.wait(ph.hw)

        ctx.phase(name, body)



def emit_l0_attn(ctx):
    X = [ctx.t[f"X{c}"] for c in range(NC)]
    wq_d, wk_d, wv_d, wo_d = (ctx.d[k] for k in ["l0a_wq", "l0a_wk", "l0a_wv", "l0a_wo"])
    es_out = contextlib.ExitStack()
    with es_out:
        ATTN = [ctx.sb(f"attn{c}", [128, T], BF16, stack=es_out) for c in range(NC)]
        es = contextlib.ExitStack()
        with es:
            q2 = ctx.sb("q2", [128, T], BF16, stack=es)
            k2 = ctx.sb("k2", [128, T], BF16, stack=es)
            qraw = ctx.sb("qraw", [128, 2 * TG], BF16, stack=es)
            rt1 = ctx.sb("rt1", [128, TG], BF16, stack=es)
            rt2 = ctx.sb("rt2", [128, TG], BF16, stack=es)
            v2 = ctx.sb("v2", [128, 32 * 2 * 65], BF16, stack=es)
            negm = ctx.sb("negm", [128, T], BF16, stack=es)
            mx8 = ctx.sb("mx8", [128, 16], F32, stack=es)
            negc = ctx.sb("negc", [128, 2], BF16, stack=es)
            ptile = [ctx.sb(f"pt{i}", [128, TG], BF16, stack=es) for i in range(3)]
            dent = ctx.sb("dent", [128, TG], F32, stack=es)
            anorm = ctx.sb("anorm", [64, TG], F32, stack=es)
            utmp = ctx.sb("utmp", [64, TG], BF16, stack=es)
            wqs = ctx.sb("wqs", [128, 8 * 128], BF16, stack=es)
            wks = ctx.sb("wks", [128, 8 * 128], BF16, stack=es)
            wvs = ctx.sb("wvs", [128, 8 * 128], BF16, stack=es)
            psQ = [ctx.ps(f"psQ{i}", [128, TG], F32, stack=es) for i in range(2)]
            psS = [ctx.ps(f"psS{i}", [128, TG], F32, stack=es) for i in range(2)]
            psB = ctx.ps("psB", [128, TG], F32, stack=es)
            psC = ctx.ps("psC", [128, TG], F32, stack=es)
            psU = ctx.ps("psU", [128, TG], F32, stack=es)
            v2v = v2[:].rearrange("p (tc hs f) -> p tc hs f", tc=32, hs=2)
            qrawv = qraw[:].rearrange("p (i f) -> p i f", i=2)
            wqsv = wqs[:].rearrange("p (n f) -> p n f", n=NC)
            wksv = wks[:].rearrange("p (n f) -> p n f", n=NC)
            wvsv = wvs[:].rearrange("p (n f) -> p n f", n=NC)

            for pair in range(8):
                def pairbody(ph, pair=pair):
                    ident = ctx.t["s_IDENT"]
                    ones = ctx.t["s_ONES"]
                    onesf = ctx.t["s_ONESF"]
                    pswap = ctx.t["s_PSWAP"]
                    rcos = ctx.t["s_RCOS"]
                    rsin = ctx.t["s_RSIN"]
                    csl = slice(pair * 128, (pair + 1) * 128)
                    ph.sw.dma(wvsv, wv_d[:, csl].rearrange("(n p) f -> p n f", p=128))
                    swv = ph.sw.n
                    ph.sw.dma(wqsv, wq_d[:, csl].rearrange("(n p) f -> p n f", p=128))
                    swq = ph.sw.n
                    ph.sw.dma(wksv, wk_d[:, csl].rearrange("(n p) f -> p n f", p=128))
                    swk = ph.sw.n
                    mset = ph.ve.op("memset", v2v[:, :, :, 64:65], 1.0)
                    evac = {}
                    for tc in range(32):
                        if tc == 0:
                            ph.pe.wait(ph.sw, swv)
                        if tc >= 2:
                            ph.pe.wait(ph.sc, evac[tc - 2])
                        for c in range(NC - 1):
                            ph.pe.raw("matmul", psQ[tc % 2][:, 0:128],
                                      X[c][:, tc * 128:(tc + 1) * 128], wvsv[:, c, :],
                                      start=(c == 0), stop=False)
                        mmd = ph.pe.op("matmul", psQ[tc % 2][:, 0:128],
                                       X[NC - 1][:, tc * 128:(tc + 1) * 128], wvsv[:, NC - 1, :],
                                       start=False, stop=True)
                        ph.sc.wait(ph.pe, mmd)
                        if tc == 0:
                            ph.sc.wait(ph.ve, mset)
                        for hs in range(2):
                            evac[tc] = ph.sc.op("activation", v2v[:, tc, hs, 0:64],
                                                psQ[tc % 2][:, hs * 64:(hs + 1) * 64], AF.Copy)
                    vproj_done = evac[31]
                    rope_done = {}
                    evac_done = {}
                    swap_done = {}
                    for qk in range(2):
                        dst = q2 if qk == 0 else k2
                        wsv = wqsv if qk == 0 else wksv
                        for tg in range(NTG):
                            i = qk * 8 + tg
                            sl = slice(tg * TG, (tg + 1) * TG)
                            if i == 0:
                                ph.pe.wait(ph.sw, swq)
                                ph.pe.wait(ph.sc, vproj_done)
                            if i == 8:
                                ph.pe.wait(ph.sw, swk)
                            if i >= 2:
                                ph.pe.wait(ph.sc, evac_done[i - 2])
                            for c in range(NC - 1):
                                ph.pe.raw("matmul", psQ[i % 2][:, :], wsv[:, c, :],
                                          X[c][:, sl], start=(c == 0), stop=False)
                            pmm = ph.pe.op("matmul", psQ[i % 2][:, :], wsv[:, NC - 1, :],
                                           X[NC - 1][:, sl], start=False, stop=True)
                            ph.sc.wait(ph.pe, pmm)
                            if i >= 2:
                                ph.sc.wait(ph.ve, rope_done[i - 2])
                                ph.sc.wait(ph.pe, swap_done["mm%d" % (i - 2)])
                            evac_done[i] = ph.sc.op("activation", qrawv[:, i % 2, :], psQ[i % 2][:, :],
                                                    AF.Copy, scale=(0.125 if qk == 0 else 1.0))
                            ph.pe.wait(ph.sc, evac_done[i])
                            if i >= 2:
                                ph.pe.wait(ph.ve, swap_done[i - 2])
                            smm = ph.pe.op("matmul", psS[i % 2][:, :], pswap[:], qrawv[:, i % 2, :],
                                           start=True, stop=True)
                            swap_done["mm%d" % i] = smm
                            if i >= 1:
                                ph.ve.wait(ph.ve, rope_done[i - 1])
                            ph.ve.wait(ph.sc, evac_done[i])
                            ph.ve.op("tensor_tensor", rt1[:], rcos[:, sl], qrawv[:, i % 2, :], ALU.mult)
                            ph.ve.wait(ph.pe, smm)
                            swap_done[i] = ph.ve.op("tensor_tensor", rt2[:], rsin[:, sl],
                                                    psS[i % 2][:, :], ALU.mult)
                            ph.ve.wait(ph.ve, swap_done[i])
                            rope_done[i] = ph.ve.op("tensor_tensor", dst[:, sl], rt1[:], rt2[:], ALU.add)
                    qk_ready = rope_done[15]
                    tmm = {}
                    flush = {}
                    red_last = {}
                    negflush = 0
                    psq_hist = {0: ("sc", evac_done[14]), 1: ("sc", evac_done[15])}

                    def mtail(hh, qt, nch, gi):
                        nonlocal negflush
                        b = 64 * hh
                        co = (gi % 2)
                        ph.ve.wait(ph.ve)
                        if gi >= 2:
                            ph.ve.wait(ph.pe, tmm[gi - 2])
                        negv = ph.ve.op("reduce_max", negc[:, co:co + 1],
                                        mx8[:, co * 8:co * 8 + nch], axis=AX.X, negate=True)
                        ph.pe.wait(ph.ve, negv)
                        grp = qt // 4
                        if qt % 4 == 0 and (hh, grp - 2) in flush:
                            ph.pe.wait(ph.sc, flush[(hh, grp - 2)])
                        nb = (psC if grp % 2 == 0 else psU)
                        tp = (0, 64) if hh else None
                        tmm[gi] = ph.pe.op("matmul", nb[b:b + 1, (qt % 4) * 128:(qt % 4 + 1) * 128],
                                           negc[:, co:co + 1], ident[:],
                                           start=True, stop=True, tile_position=tp)
                        if qt % 4 == 3:
                            ph.sc.wait(ph.pe, tmm[gi])
                            flush[(hh, grp)] = ph.sc.op("activation",
                                                        negm[b:b + 1, grp * 512:(grp + 1) * 512],
                                                        nb[b:b + 1, :], AF.Copy)
                            negflush = flush[(hh, grp)]

                    gi = 0
                    pend = None
                    for hh in range(2):
                        b = 64 * hh
                        for qt in range(32):
                            nch = qt // 4 + 1
                            for kc in range(nch):
                                if (hh, qt, kc) == (0, 0, 0):
                                    ph.pe.wait(ph.ve, qk_ready)
                                q_, n_ = psq_hist[kc % 2]
                                ph.pe.wait(q_, n_)
                                mm = ph.pe.op("matmul", psQ[kc % 2][:, :],
                                              q2[b:b + 64, qt * 128:(qt + 1) * 128],
                                              k2[b:b + 64, kc * 512:(kc + 1) * 512],
                                              start=True, stop=True)
                                ph.ve.wait(ph.pe, mm)
                                if kc == 0:
                                    ph.ve.wait(ph.ve)
                                n = ph.ve.op("reduce_max", mx8[:, (gi % 2) * 8 + kc:(gi % 2) * 8 + kc + 1],
                                             psQ[kc % 2][:, :], axis=AX.X)
                                psq_hist[kc % 2] = ("ve", n)
                            red_last[gi] = n
                            if pend is not None:
                                mtail(*pend)
                            pend = (hh, qt, nch, gi)
                            gi += 1
                    mtail(*pend)
                    ev_prev = [0]
                    att_prev = 0
                    up_prev = 0
                    pss_hist = {0: swap_done[14], 1: swap_done[15]}
                    pss_q = {0: "ve", 1: "ve"}
                    pv_last = 0
                    for hh in range(2):
                        b = 64 * hh
                        for qg in range(NTG):
                            sl = slice(qg * TG, (qg + 1) * TG)
                            kcmax = 4 * qg + 4
                            expd = {}
                            pvmm = {}

                            def emit_pv(kc, kcmax=kcmax, hh=hh, qg=qg):
                                ph.pe.wait(ph.sc, expd[kc])
                                if kc == 0:
                                    if vproj_done and (hh, qg) == (0, 0):
                                        ph.pe.wait(ph.sc, vproj_done)
                                    if ev_prev[0]:
                                        ph.pe.wait(ph.sc, ev_prev[0])
                                pvmm[kc] = ph.pe.op("matmul", psB[0:65, :], v2v[:, kc, hh, :],
                                                    ptile[kc % 3][:], start=(kc == 0),
                                                    stop=(kc == kcmax - 1))

                            for kc in range(kcmax):
                                if (hh, qg, kc) == (0, 0, 0):
                                    ph.pe.wait(ph.ve, qk_ready)
                                    ph.pe.wait(ph.sc, negflush)
                                if pss_hist[kc % 2]:
                                    ph.pe.wait(pss_q[kc % 2], pss_hist[kc % 2])
                                ph.pe.raw("matmul", psS[kc % 2][:, :],
                                          k2[b:b + 64, kc * 128:(kc + 1) * 128], q2[b:b + 64, sl],
                                          start=True, stop=False)
                                stmm = ph.pe.op("matmul", psS[kc % 2][:, :],
                                                ones[b:b + 1, 0:128], negm[b:b + 1, sl],
                                                start=False, stop=True)
                                diag = kc >= kcmax - 4
                                if diag:
                                    ph.ve.wait(ph.pe, stmm)
                                    mk = ph.ve.op("tensor_tensor", psS[kc % 2][:, :], psS[kc % 2][:, :],
                                                  cv(ctx, "CAUSAL", kc - 4 * qg), ALU.add)
                                    ph.sc.wait(ph.ve, mk)
                                else:
                                    ph.sc.wait(ph.pe, stmm)
                                if kc >= 3:
                                    ph.sc.wait(ph.pe, pvmm[kc - 3])
                                elif pv_last:
                                    ph.sc.wait(ph.pe, pv_last)
                                expd[kc] = ph.sc.op("activation", ptile[kc % 3][:], psS[kc % 2][:, :], AF.Exp)
                                pss_hist[kc % 2] = expd[kc]
                                pss_q[kc % 2] = "sc"
                                if kc >= 1:
                                    emit_pv(kc - 1)
                            emit_pv(kcmax - 1)
                            pv_last = pvmm[kcmax - 1]
                            ph.sc.wait(ph.pe, pvmm[kcmax - 1])
                            d1 = ph.sc.op("activation", dent[64:65, :], psB[64:65, :], AF.Copy)
                            ph.ve.wait(ph.sc, d1)
                            d2 = ph.ve.op("reciprocal", dent[64:65, :], dent[64:65, :])
                            ph.pe.wait(ph.ve, d2)
                            if att_prev:
                                ph.pe.wait(ph.ve, att_prev)
                            bcmm = ph.pe.op("matmul", psC[0:64, :], onesf[64:65, 0:64],
                                            dent[64:65, :], start=True, stop=True)
                            ph.sc.wait(ph.pe, pvmm[kcmax - 1])
                            ev_prev[0] = ph.sc.op("activation", anorm[:], psB[0:64, :], AF.Copy)
                            ph.ve.wait(ph.sc, ev_prev[0])
                            ph.ve.wait(ph.pe, bcmm)
                            if hh == 0:
                                att_prev = ph.ve.op("tensor_tensor", ATTN[pair][0:64, sl],
                                                    anorm[:], psC[0:64, :], ALU.mult)
                            else:
                                if up_prev:
                                    ph.ve.wait(ph.sc, up_prev)
                                att_prev = ph.ve.op("tensor_tensor", utmp[:], anorm[:],
                                                    psC[0:64, :], ALU.mult)
                                ph.pe.wait(ph.ve, att_prev)
                                umm = ph.pe.op("matmul", psU[64:128, :], ident[0:64, 0:64],
                                               utmp[:], start=True, stop=True, tile_position=(0, 64))
                                ph.sc.wait(ph.pe, umm)
                                up_prev = ph.sc.op("activation", ATTN[pair][64:128, sl],
                                                   psU[64:128, :], AF.Copy)
                ctx.phase(f"l0a_p{pair}", pairbody)

        wos = ctx.sb("wos", [128, 8 * 1024], BF16, stack=es_out)
        wosv = wos[:].rearrange("p (n f) -> p n f", n=NC)

        def wo_load(ph):
            ph.sw.dma(wosv, wo_d.rearrange("(n p) f -> p n f", p=128))
            ph.sw.wait(ph.sw)
        ctx.phase("l0a_wol", wo_load)

        psW = [ctx.ps(f"l0aW{i}", [128, TG], F32, stack=es_out) for i in range(2)]

        stt_hist = {}

        def produce(ph, tg, vsl, marks, prev):
            for c in range(NC):
                if c >= 2:
                    ph.pe.wait(ph.ve, stt_hist[(tg, c - 2)])
                elif tg > 0:
                    ph.pe.wait(ph.ve, stt_hist[(tg - 1, c + 6)])
                for ic in range(NC - 1):
                    ph.pe.raw("matmul", psW[c % 2][:, :], wosv[:, ic, c * 128:(c + 1) * 128],
                              ATTN[ic][:, tg * TG:(tg + 1) * TG], start=(ic == 0), stop=False)
                mmd = ph.pe.op("matmul", psW[c % 2][:, :], wosv[:, NC - 1, c * 128:(c + 1) * 128],
                               ATTN[NC - 1][:, tg * TG:(tg + 1) * TG], start=False, stop=True)
                ph.ve.wait(ph.pe, mmd)
                if prev is not None:
                    ph.ve.wait(ph.ve, prev["app"][c])
                    ph.ve.wait(ph.pe, prev["summ"])
                    ph.ve.wait(ph.sc, prev["sq"][c])
                stt_hist[(tg, c)] = ph.ve.op("scalar_tensor_tensor", vsl[c][:], psW[c % 2][:, :], 1.0,
                                             X[c][:, tg * TG:(tg + 1) * TG], ALU.mult, ALU.add)
                marks[(tg, c)] = ("ve", stt_hist[(tg, c)])

        ln_finish(ctx, "l0a_ln", produce)



def emit_l1_ffn(ctx):
    X = [ctx.t[f"X{c}"] for c in range(NC)]
    kern_d = ctx.d["l1f_kernel"]
    a_d = ctx.d["l1f_a"]
    b_d = ctx.d["l1f_b"]
    bias_d = ctx.d["l1f_bias"]
    es = contextlib.ExitStack()
    with es:
        kf = ctx.sb("l1f_kf", [128, 8 * 1024], F32, stack=es)
        wq = ctx.sb("l1f_wq", [128, 8 * 1024], BF16, stack=es)
        a_s = ctx.sb("l1f_as", [128, 8 * 16], BF16, stack=es)
        b_s = ctx.sb("l1f_bs", [16, 1024], BF16, stack=es)
        bias_s = ctx.sb("l1f_biass", [128, 8], F32, stack=es)
        t0 = ctx.sb("l1f_t0", [128, 1024], F32, stack=es)
        ti = ctx.sb("l1f_ti", [128, 1024], I32, stack=es)
        tf = ctx.sb("l1f_tf", [128, 1024], F32, stack=es)
        td = ctx.sb("l1f_td", [128, 1024], F32, stack=es)
        mcol = ctx.sb("l1f_mc", [128, 1], F32, stack=es)
        mrow = ctx.sb("l1f_mr", [1, 128], F32, stack=es)
        sc1 = ctx.sb("l1f_s1", [1, 1], F32, stack=es)
        invs = ctx.sb("l1f_iv", [128, 1], F32, stack=es)
        sval = ctx.sb("l1f_sv", [128, 1], F32, stack=es)
        t1 = ctx.sb("l1f_t1", [16, TG], BF16, stack=es)
        psP = ctx.ps("l1f_p0", [128, TG], F32, stack=es)
        kfv = kf[:].rearrange("p (n f) -> p n f", n=NC)
        wqv = wq[:].rearrange("p (n f) -> p n f", n=NC)
        asv = a_s[:].rearrange("p (n f) -> p n f", n=NC)

        def qphase(ph):
            identf = ctx.t["s_IDENTF"]
            onesf = ctx.t["s_ONESF"]
            ph.hw.dma(kfv, kern_d.rearrange("(n p) f -> p n f", p=128))
            ph.sw.dma(asv, a_d.rearrange("(n p) f -> p n f", p=128))
            ph.sw.dma(b_s[:], b_d)
            ph.hw.dma(bias_s[:], bias_d)
            ph.ve.wait(ph.hw)
            n = ph.ve.op("reduce_max", mcol[:], kf[:], axis=AX.X, apply_absolute_value=True)
            ph.pe.wait(ph.ve, n)
            tm = ph.pe.op("matmul", psP[0:1, 0:128], mcol[:], identf[:], start=True, stop=True)
            ph.ve.wait(ph.pe, tm)
            n = ph.ve.op("tensor_copy", mrow[:], psP[0:1, 0:128])
            ph.ve.wait(ph.ve, n)
            n = ph.ve.op("reduce_max", sc1[:], mrow[:], axis=AX.X)
            ph.ve.wait(ph.ve, n)
            n = ph.ve.op("reciprocal", sc1[:], sc1[:])
            ph.pe.wait(ph.ve, n)
            bc = ph.pe.op("matmul", psP[0:128, 0:1], onesf[0:1, :], sc1[:], start=True, stop=True)
            ph.ve.wait(ph.pe, bc)
            n = ph.ve.op("tensor_scalar_mul", invs[:], psP[0:128, 0:1], 7.0)
            ph.ve.wait(ph.ve, n)
            n = ph.ve.op("reciprocal", sval[:], invs[:])
            ph.ve.wait(ph.ve, n)
            scales_ready = n
            prev_ve_end = 0
            prev_wq = 0
            for c in range(NC):
                if c == 0:
                    ph.sc.wait(ph.hw)
                    ph.sc.wait(ph.ve, scales_ready)
                if prev_ve_end:
                    ph.sc.wait(ph.ve, prev_ve_end)
                m = ph.sc.op("activation", t0[:], kfv[:, c, :], AF.Copy, scale=invs[:, 0:1])
                ph.ve.wait(ph.sc, m)
                n = ph.ve.op("tensor_copy", ti[:], t0[:])
                ph.ve.wait(ph.ve, n)
                if prev_wq:
                    ph.ve.wait(ph.sc, prev_wq)
                n = ph.ve.op("tensor_copy", tf[:], ti[:])
                ph.ve.wait(ph.ve, n)
                n = ph.ve.op("tensor_tensor", td[:], t0[:], tf[:], ALU.subtract)
                ph.ve.wait(ph.ve, n)
                n = ph.ve.op("tensor_single_scalar", t0[:], td[:], 0.5, ALU.is_gt)
                ph.ve.wait(ph.ve, n)
                n = ph.ve.op("tensor_tensor", tf[:], tf[:], t0[:], ALU.add)
                ph.ve.wait(ph.ve, n)
                n = ph.ve.op("tensor_single_scalar", t0[:], td[:], -0.5, ALU.is_lt)
                ph.ve.wait(ph.ve, n)
                n = ph.ve.op("tensor_tensor", tf[:], tf[:], t0[:], ALU.subtract)
                ph.ve.wait(ph.ve, n)
                n = ph.ve.op("tensor_scalar_min", tf[:], tf[:], 7.0)
                ph.ve.wait(ph.ve, n)
                n = ph.ve.op("tensor_scalar_max", tf[:], tf[:], -8.0)
                prev_ve_end = n
                ph.sc.wait(ph.ve, n)
                prev_wq = ph.sc.op("activation", wqv[:, c, :], tf[:], AF.Copy, scale=sval[:, 0:1])
        ctx.phase("l1f_q", qphase)

        if DEBUG_L1F:
            wqo = ctx.dram_out("dbg_wq", [128, 8 * 1024], BF16)

            def dump(ph):
                ph.hw.dma(wqo, wq[:])
                ph.hw.wait(ph.hw)
            ctx.phase("l1f_dbg", dump)

        psW = [ctx.ps(f"l1fW{i}", [128, TG], F32, stack=es) for i in range(2)]
        psL = ctx.ps("l1fL", [16, TG], F32, stack=es)
        stt_hist = {}

        def produce(ph, tg, vsl, marks, prev):
            if tg == 0:
                ph.pe.wait(ph.sw)
            if not L1F_SKIP_LORA:
                if tg > 0:
                    ph.pe.wait(ph.sc, stt_hist[("t1", tg - 1)])
                for c in range(NC - 1):
                    ph.pe.raw("matmul", psL[:, :], asv[:, c, :], X[c][:, tg * TG:(tg + 1) * TG],
                              start=(c == 0), stop=False)
                lm = ph.pe.op("matmul", psL[:, :], asv[:, NC - 1, :], X[NC - 1][:, tg * TG:(tg + 1) * TG],
                              start=False, stop=True)
                if tg > 0:
                    ph.sc.wait(ph.pe, stt_hist[("l2", tg - 1)])
                ph.sc.wait(ph.pe, lm)
                t1d = ph.sc.op("activation", t1[:], psL[:, :], AF.Copy)
                stt_hist[("t1", tg)] = t1d
            for c in range(NC):
                if c >= 2:
                    ph.pe.wait(ph.ve, stt_hist[(tg, c - 2)])
                elif tg > 0:
                    ph.pe.wait(ph.ve, stt_hist[(tg - 1, c + 6)])
                for ic in range(NC - 1):
                    ph.pe.raw("matmul", psW[c % 2][:, :], wqv[:, ic, c * 128:(c + 1) * 128],
                              X[ic][:, tg * TG:(tg + 1) * TG], start=(ic == 0), stop=False)
                if L1F_SKIP_LORA:
                    mmd = ph.pe.op("matmul", psW[c % 2][:, :], wqv[:, NC - 1, c * 128:(c + 1) * 128],
                                   X[NC - 1][:, tg * TG:(tg + 1) * TG], start=False, stop=True)
                    stt_hist[("l2", tg)] = mmd
                else:
                    ph.pe.raw("matmul", psW[c % 2][:, :], wqv[:, NC - 1, c * 128:(c + 1) * 128],
                              X[NC - 1][:, tg * TG:(tg + 1) * TG], start=False, stop=False)
                    ph.pe.wait(ph.sc, t1d)
                    mmd = ph.pe.op("matmul", psW[c % 2][:, :], b_s[:, c * 128:(c + 1) * 128],
                                   t1[:], start=False, stop=True)
                if c == NC - 1:
                    stt_hist[("l2", tg)] = mmd
                ph.ve.wait(ph.pe, mmd)
                if prev is not None:
                    ph.ve.wait(ph.ve, prev["app"][c])
                    ph.ve.wait(ph.pe, prev["summ"])
                    ph.ve.wait(ph.sc, prev["sq"][c])
                stt_hist[(tg, c)] = ph.ve.op("scalar_tensor_tensor", vsl[c][:], psW[c % 2][:, :],
                                             bias_s[:, c:c + 1], X[c][:, tg * TG:(tg + 1) * TG],
                                             ALU.add, ALU.add)
                marks[(tg, c)] = ("ve", stt_hist[(tg, c)])

        ln_finish(ctx, "l1f_ln", produce)



def emit_l2_attn(ctx):
    X = [ctx.t[f"X{c}"] for c in range(NC)]
    wq_d, wk_d, wv_d, wo_d = (ctx.d[k] for k in ["l2a_wq", "l2a_wk", "l2a_wv", "l2a_wo"])
    bo_d = ctx.d["l2a_bo"]
    es_out = contextlib.ExitStack()
    with es_out:
        ATTN = [ctx.sb(f"l2attn{c}", [128, T], BF16, stack=es_out) for c in range(NC)]
        bo_s = ctx.sb("l2a_bos", [128, 8], F32, stack=es_out)
        es = contextlib.ExitStack()
        with es:
            ktm = ctx.sb("l2a_ktm", [128, 32 * 256], BF16, stack=es)
            v4 = ctx.sb("l2a_v4", [128, 32 * 4 * 65], BF16, stack=es)
            q2 = ctx.sb("l2a_q2", [128, T], BF16, stack=es)
            kvs = ctx.sb("l2a_kvs", [128, 65], BF16, stack=es)
            e_t0 = ctx.sb("l2a_e0", [128, TG], F32, stack=es)
            e_t1 = ctx.sb("l2a_e1", [128, TG], F32, stack=es)
            e_t2 = ctx.sb("l2a_e2", [128, TG], F32, stack=es)
            dent = ctx.sb("l2a_dent", [128, TG], F32, stack=es)
            anorm = ctx.sb("l2a_an", [64, TG], F32, stack=es)
            utmp = ctx.sb("l2a_ut", [64, TG], BF16, stack=es)
            wqs = ctx.sb("l2a_wqs", [128, 8 * 128], BF16, stack=es)
            wks = ctx.sb("l2a_wks", [128, 8 * 256], BF16, stack=es)
            wvs = ctx.sb("l2a_wvs", [128, 8 * 256], BF16, stack=es)
            psQ = [ctx.ps(f"l2aQ{i}", [128, TG], F32, stack=es) for i in range(2)]
            psB = ctx.ps("l2aB", [128, TG], F32, stack=es)
            psC = ctx.ps("l2aC", [128, TG], F32, stack=es)
            psU = ctx.ps("l2aU", [128, TG], F32, stack=es)
            ktmv = ktm[:].rearrange("p (tc f) -> p tc f", tc=32)
            v4v = v4[:].rearrange("p (tc hs f) -> p tc hs f", tc=32, hs=4)
            wqsv = wqs[:].rearrange("p (n f) -> p n f", n=NC)
            wksv = wks[:].rearrange("p (n f) -> p n f", n=NC)
            wvsv = wvs[:].rearrange("p (n f) -> p n f", n=NC)

            def elu_chain(ph, raw_ap, out_ap, t0, t1, guards):
                n = ph.ve.op("tensor_scalar_min", t0[:], raw_ap, 0.0)
                ph.sc.wait(ph.ve, n)
                m = ph.sc.op("activation", t1[:], t0[:], AF.Exp)
                n2 = ph.ve.op("tensor_scalar_max", t0[:], raw_ap, 0.0)
                ph.ve.wait(ph.sc, m)
                ph.ve.wait(ph.ve, n2)
                return ph.ve.op("tensor_tensor", out_ap, t1[:], t0[:], ALU.add)

            for g in range(4):
                def groupbody(ph, g=g):
                    ph.sw.dma(wksv, wk_d[:, g * 256:(g + 1) * 256].rearrange("(n p) f -> p n f", p=128))
                    ph.sw.dma(wvsv, wv_d[:, g * 256:(g + 1) * 256].rearrange("(n p) f -> p n f", p=128))
                    swkv = ph.sw.n
                    mset = ph.ve.op("memset", v4v[:, :, :, 64:65], 1.0)
                    ev = {}
                    for tc in range(32):
                        for kv in range(2):
                            j = tc * 2 + kv
                            if j == 0:
                                ph.pe.wait(ph.sw, swkv)
                            if j >= 2:
                                ph.pe.wait(ph.ve if (j % 2 == 0) else ph.sc, ev[j - 2])
                            wsv = wksv if kv == 0 else wvsv
                            for c in range(NC - 1):
                                ph.pe.raw("matmul", psQ[j % 2][:, 0:256],
                                          X[c][:, tc * 128:(tc + 1) * 128], wsv[:, c, :],
                                          start=(c == 0), stop=False)
                            mmd = ph.pe.op("matmul", psQ[j % 2][:, 0:256],
                                           X[NC - 1][:, tc * 128:(tc + 1) * 128], wsv[:, NC - 1, :],
                                           start=False, stop=True)
                            if kv == 0:
                                ph.sc.wait(ph.pe, mmd)
                                if tc > 0:
                                    ph.sc.wait(ph.ve, ev[j - 2])
                                m = ph.sc.op("activation", e_t2[:, 0:256], psQ[j % 2][:, 0:256], AF.Copy)
                                ph.ve.wait(ph.sc, m)
                                ev[j] = elu_chain(ph, e_t2[:, 0:256], ktmv[:, tc, :],
                                                  e_t0[:, 0:256][0:128, :], e_t1[:, 0:256][0:128, :], None)
                            else:
                                ph.sc.wait(ph.pe, mmd)
                                if j == 1:
                                    ph.sc.wait(ph.ve, mset)
                                for hs in range(4):
                                    ev[j] = ph.sc.op("activation", v4v[:, tc, hs, 0:64],
                                                     psQ[j % 2][:, hs * 64:(hs + 1) * 64], AF.Copy)
                    kv_ready = ev[63]
                    kf_ready = ev[62]
                    for pp in range(2):
                        pair = 2 * g + pp
                        ph.sw.dma(wqsv, wq_d[:, pair * 128:(pair + 1) * 128]
                                  .rearrange("(n p) f -> p n f", p=128))
                        swq = ph.sw.n
                        rope_done = {}
                        for tg in range(NTG):
                            sl = slice(tg * TG, (tg + 1) * TG)
                            if tg == 0:
                                ph.pe.wait(ph.sw, swq)
                                ph.pe.wait(ph.sc, kv_ready)
                                ph.pe.wait(ph.ve, kf_ready)
                            if tg >= 2:
                                ph.pe.wait(ph.ve, rope_done[tg - 2])
                            for c in range(NC - 1):
                                ph.pe.raw("matmul", psQ[tg % 2][:, :], wqsv[:, c, :],
                                          X[c][:, sl], start=(c == 0), stop=False)
                            pmm = ph.pe.op("matmul", psQ[tg % 2][:, :], wqsv[:, NC - 1, :],
                                           X[NC - 1][:, sl], start=False, stop=True)
                            ph.sc.wait(ph.pe, pmm)
                            if tg >= 2:
                                ph.sc.wait(ph.ve, rope_done[tg - 2])
                            m = ph.sc.op("activation", e_t2[:], psQ[tg % 2][:, :], AF.Copy)
                            ph.ve.wait(ph.sc, m)
                            if tg >= 1:
                                ph.ve.wait(ph.ve, rope_done[tg - 1])
                            rope_done[tg] = elu_chain(ph, e_t2[:], q2[:, sl], e_t0, e_t1, None)
                        qf_ready = rope_done[NTG - 1]
                        ev_prev = 0
                        att_prev = 0
                        up_prev = 0
                        kvev_prev = 0
                        for hh in range(2):
                            b = 64 * hh
                            hs = 2 * pp + hh
                            if kvev_prev:
                                ph.pe.wait(ph.sc, kvev_prev)
                            ph.pe.wait(ph.ve, qf_ready)
                            for tc in range(32 - 1):
                                ph.pe.raw("matmul", psU[b:b + 64, 0:65] if hh else psU[0:64, 0:65],
                                          ktmv[:, tc, hs * 64:(hs + 1) * 64], v4v[:, tc, hs, :],
                                          start=(tc == 0), stop=False,
                                          tile_position=((0, 64) if hh else None))
                            kmm = ph.pe.op("matmul", psU[b:b + 64, 0:65] if hh else psU[0:64, 0:65],
                                           ktmv[:, 31, hs * 64:(hs + 1) * 64], v4v[:, 31, hs, :],
                                           start=False, stop=True,
                                           tile_position=((0, 64) if hh else None))
                            ph.sc.wait(ph.pe, kmm)
                            kvev_prev = ph.sc.op("activation", kvs[b:b + 64, :], psU[b:b + 64, 0:65], AF.Copy)
                            for qg in range(NTG):
                                sl = slice(qg * TG, (qg + 1) * TG)
                                if ev_prev:
                                    ph.pe.wait(ph.sc, ev_prev)
                                ph.pe.wait(ph.sc, kvev_prev)
                                qkmm = ph.pe.op("matmul", psB[0:65, :], kvs[b:b + 64, :],
                                                q2[b:b + 64, sl], start=True, stop=True)
                                ph.sc.wait(ph.pe, qkmm)
                                d1 = ph.sc.op("activation", dent[64:65, :], psB[64:65, :], AF.Copy)
                                ph.ve.wait(ph.sc, d1)
                                n = ph.ve.op("tensor_scalar_add", dent[64:65, :], dent[64:65, :], 1e-6)
                                ph.ve.wait(ph.ve, n)
                                d2 = ph.ve.op("reciprocal", dent[64:65, :], dent[64:65, :])
                                ph.pe.wait(ph.ve, d2)
                                if att_prev:
                                    ph.pe.wait(ph.ve, att_prev)
                                bcmm = ph.pe.op("matmul", psC[0:64, :], onesf_g[64:65, 0:64],
                                                dent[64:65, :], start=True, stop=True)
                                ev_prev = ph.sc.op("activation", anorm[:], psB[0:64, :], AF.Copy)
                                ph.ve.wait(ph.sc, ev_prev)
                                ph.ve.wait(ph.pe, bcmm)
                                if hh == 0:
                                    att_prev = ph.ve.op("tensor_tensor", ATTN[pair][0:64, sl],
                                                        anorm[:], psC[0:64, :], ALU.mult)
                                else:
                                    if up_prev:
                                        ph.ve.wait(ph.sc, up_prev)
                                    att_prev = ph.ve.op("tensor_tensor", utmp[:], anorm[:],
                                                        psC[0:64, :], ALU.mult)
                                    ph.pe.wait(ph.ve, att_prev)
                                    umm = ph.pe.op("matmul", psC[64:128, :], ident_g[0:64, 0:64],
                                                   utmp[:], start=True, stop=True, tile_position=(0, 64))
                                    ph.sc.wait(ph.pe, umm)
                                    up_prev = ph.sc.op("activation", ATTN[pair][64:128, sl],
                                                       psC[64:128, :], AF.Copy)
                ident_g = ctx.t["s_IDENT"]
                onesf_g = ctx.t["s_ONESF"]
                ctx.phase(f"l2a_g{g}", groupbody)

        wos = ctx.sb("l2a_wos", [128, 8 * 1024], BF16, stack=es_out)
        wosv = wos[:].rearrange("p (n f) -> p n f", n=NC)

        def wo_load(ph):
            ph.sw.dma(wosv, wo_d.rearrange("(n p) f -> p n f", p=128))
            ph.hw.dma(bo_s[:], bo_d)
            ph.sw.wait(ph.sw)
            ph.hw.wait(ph.hw)
        ctx.phase("l2a_wol", wo_load)

        psW = [ctx.ps(f"l2aW{i}", [128, TG], F32, stack=es_out) for i in range(2)]
        stt_hist = {}

        def produce(ph, tg, vsl, marks, prev):
            for c in range(NC):
                if c >= 2:
                    ph.pe.wait(ph.ve, stt_hist[(tg, c - 2)])
                elif tg > 0:
                    ph.pe.wait(ph.ve, stt_hist[(tg - 1, c + 6)])
                for ic in range(NC - 1):
                    ph.pe.raw("matmul", psW[c % 2][:, :], wosv[:, ic, c * 128:(c + 1) * 128],
                              ATTN[ic][:, tg * TG:(tg + 1) * TG], start=(ic == 0), stop=False)
                mmd = ph.pe.op("matmul", psW[c % 2][:, :], wosv[:, NC - 1, c * 128:(c + 1) * 128],
                               ATTN[NC - 1][:, tg * TG:(tg + 1) * TG], start=False, stop=True)
                ph.ve.wait(ph.pe, mmd)
                if tg == 0 and c == 0:
                    ph.ve.wait(ph.hw)
                if prev is not None:
                    ph.ve.wait(ph.ve, prev["app"][c])
                    ph.ve.wait(ph.pe, prev["summ"])
                    ph.ve.wait(ph.sc, prev["sq"][c])
                stt_hist[(tg, c)] = ph.ve.op("scalar_tensor_tensor", vsl[c][:], psW[c % 2][:, :],
                                             bo_s[:, c:c + 1], X[c][:, tg * TG:(tg + 1) * TG],
                                             ALU.add, ALU.add)
                marks[(tg, c)] = ("ve", stt_hist[(tg, c)])

        ln_finish(ctx, "l2a_ln", produce)



def emit_l3_attn(ctx):
    X = [ctx.t[f"X{c}"] for c in range(NC)]
    wq_d, wk_d, wv_d, wo_d = (ctx.d[k] for k in ["l3a_wq", "l3a_wk", "l3a_wv", "l3a_wo"])
    dirs_d = ctx.d["l3a_dirs"]
    hs_d = ctx.d["l3a_hs"]
    es_out = contextlib.ExitStack()
    with es_out:
        ATTN = [ctx.sb(f"l3attn{c}", [128, T], BF16, stack=es_out) for c in range(NC)]
        es = contextlib.ExitStack()
        with es:
            q2 = ctx.sb("l3q2", [128, T], BF16, stack=es)
            k2 = ctx.sb("l3k2", [128, T], BF16, stack=es)
            v2 = ctx.sb("l3v2", [128, 32 * 2 * 65], BF16, stack=es)
            qgeo = ctx.sb("l3qg", [128, T], BF16, stack=es)
            kgeo = ctx.sb("l3kg", [128, T], BF16, stack=es)
            negm = ctx.sb("l3negm", [128, T], BF16, stack=es)
            mx8 = ctx.sb("l3mx8", [128, 16], F32, stack=es)
            negc = ctx.sb("l3negc", [128, 2], BF16, stack=es)
            dirs = ctx.sb("l3dirs", [128, 16], BF16, stack=es)
            hsr = ctx.sb("l3hsr", [1, 16], F32, stack=es)
            hs128 = ctx.sb("l3hs128", [128, 16], F32, stack=es)
            ptile = [ctx.sb(f"l3pt{i}", [128, TG], BF16, stack=es) for i in range(3)]
            dent = ctx.sb("l3dent", [128, TG], F32, stack=es)
            anorm = ctx.sb("l3an", [64, TG], F32, stack=es)
            utmp = ctx.sb("l3ut", [64, TG], BF16, stack=es)
            wqs = ctx.sb("l3wqs", [128, 8 * 128], BF16, stack=es)
            wks = ctx.sb("l3wks", [128, 8 * 128], BF16, stack=es)
            wvs = ctx.sb("l3wvs", [128, 8 * 128], BF16, stack=es)
            psQ = [ctx.ps(f"l3Q{i}", [128, TG], F32, stack=es) for i in range(2)]
            psS = [ctx.ps(f"l3S{i}", [128, TG], F32, stack=es) for i in range(2)]
            psB = ctx.ps("l3B", [128, TG], F32, stack=es)
            psC = ctx.ps("l3C", [128, TG], F32, stack=es)
            psU = ctx.ps("l3U", [128, TG], F32, stack=es)
            v2v = v2[:].rearrange("p (tc hs f) -> p tc hs f", tc=32, hs=2)
            wqsv = wqs[:].rearrange("p (n f) -> p n f", n=NC)
            wksv = wks[:].rearrange("p (n f) -> p n f", n=NC)
            wvsv = wvs[:].rearrange("p (n f) -> p n f", n=NC)

            def setup(ph):
                onesf = ctx.t["s_ONESF"]
                ph.sw.dma(dirs[:], dirs_d)
                ph.hw.dma(hsr[:], hs_d)
                ph.ve.wait(ph.hw)
                n = ph.ve.op("tensor_scalar_mul", hsr[:], hsr[:], 8.0)
                ph.pe.wait(ph.ve, n)
                bm = ph.pe.op("matmul", psB[:, 0:16], onesf[0:1, :], hsr[:], start=True, stop=True)
                ph.sc.wait(ph.pe, bm)
                ph.sc.op("activation", hs128[:], psB[:, 0:16], AF.Copy)
                ph.gp.wait(ph.sw)
            ctx.phase("l3a_setup", setup)

            for pair in range(8):
                def pairbody(ph, pair=pair):
                    ident = ctx.t["s_IDENT"]
                    ones = ctx.t["s_ONES"]
                    onesf = ctx.t["s_ONESF"]
                    csl = slice(pair * 128, (pair + 1) * 128)
                    ph.sw.dma(wvsv, wv_d[:, csl].rearrange("(n p) f -> p n f", p=128))
                    swv = ph.sw.n
                    ph.sw.dma(wqsv, wq_d[:, csl].rearrange("(n p) f -> p n f", p=128))
                    swq = ph.sw.n
                    ph.sw.dma(wksv, wk_d[:, csl].rearrange("(n p) f -> p n f", p=128))
                    swk = ph.sw.n
                    mset = ph.ve.op("memset", v2v[:, :, :, 64:65], 1.0)
                    evac = {}
                    for tc in range(32):
                        if tc == 0:
                            ph.pe.wait(ph.sw, swv)
                        if tc >= 2:
                            ph.pe.wait(ph.sc, evac[tc - 2])
                        for c in range(NC - 1):
                            ph.pe.raw("matmul", psQ[tc % 2][:, 0:128],
                                      X[c][:, tc * 128:(tc + 1) * 128], wvsv[:, c, :],
                                      start=(c == 0), stop=False)
                        mmd = ph.pe.op("matmul", psQ[tc % 2][:, 0:128],
                                       X[NC - 1][:, tc * 128:(tc + 1) * 128], wvsv[:, NC - 1, :],
                                       start=False, stop=True)
                        ph.sc.wait(ph.pe, mmd)
                        if tc == 0:
                            ph.sc.wait(ph.ve, mset)
                        for hs in range(2):
                            evac[tc] = ph.sc.op("activation", v2v[:, tc, hs, 0:64],
                                                psQ[tc % 2][:, hs * 64:(hs + 1) * 64], AF.Copy)
                    vproj_done = evac[31]
                    evac_done = {}
                    for qk in range(2):
                        dst = q2 if qk == 0 else k2
                        wsv = wqsv if qk == 0 else wksv
                        for tg in range(NTG):
                            i = qk * 8 + tg
                            sl = slice(tg * TG, (tg + 1) * TG)
                            if i == 0:
                                ph.pe.wait(ph.sw, swq)
                                ph.pe.wait(ph.sc, vproj_done)
                            if i == 8:
                                ph.pe.wait(ph.sw, swk)
                            if i >= 2:
                                ph.pe.wait(ph.sc, evac_done[i - 2])
                            for c in range(NC - 1):
                                ph.pe.raw("matmul", psQ[i % 2][:, :], wsv[:, c, :],
                                          X[c][:, sl], start=(c == 0), stop=False)
                            pmm = ph.pe.op("matmul", psQ[i % 2][:, :], wsv[:, NC - 1, :],
                                           X[NC - 1][:, sl], start=False, stop=True)
                            ph.sc.wait(ph.pe, pmm)
                            evac_done[i] = ph.sc.op("activation", dst[:, sl], psQ[i % 2][:, :],
                                                    AF.Copy, scale=(0.125 if qk == 0 else 1.0))
                    qk_ready_sc = evac_done[15]
                    geo_ev = 0
                    gmm_prev = 0
                    for qk in range(2):
                        src = q2 if qk == 0 else k2
                        gdst = qgeo if qk == 0 else kgeo
                        for hh in range(2):
                            b = 64 * hh
                            h = 2 * pair + hh
                            for tg in range(NTG):
                                sl = slice(tg * TG, (tg + 1) * TG)
                                j = ((qk * 2 + hh) * 8 + tg)
                                if j == 0:
                                    ph.pe.wait(ph.sc, qk_ready_sc)
                                    ph.pe.wait(ph.gp)
                                if geo_ev:
                                    ph.pe.wait(ph.sc, geo_ev)
                                tp = (0, 64) if hh else None
                                gmm = ph.pe.op("matmul", psS[j % 2][b:b + 1, 0:TG],
                                               dirs[:, h:h + 1], src[:, sl],
                                               start=True, stop=True, tile_position=tp)
                                ph.sc.wait(ph.pe, gmm)
                                sc_ap = hs128[b:b + 1, h:h + 1] if qk == 0 else 1.0
                                if j >= 2 or True:
                                    pass
                                geo_ev = ph.sc.op("activation", gdst[b:b + 1, sl],
                                                  psS[j % 2][b:b + 1, 0:TG], AF.Copy, scale=sc_ap)
                    geo_done = geo_ev
                    tmm = {}
                    flush = {}
                    negflush = 0
                    psq_hist = {0: ("sc", evac_done[14]), 1: ("sc", evac_done[15])}

                    def mtail(hh, qt, nch, gi):
                        nonlocal negflush
                        b = 64 * hh
                        co = (gi % 2)
                        ph.ve.wait(ph.ve)
                        if gi >= 2:
                            ph.ve.wait(ph.pe, tmm[gi - 2])
                        negv = ph.ve.op("reduce_max", negc[:, co:co + 1],
                                        mx8[:, co * 8:co * 8 + nch], axis=AX.X, negate=True)
                        ph.pe.wait(ph.ve, negv)
                        grp = qt // 4
                        if qt % 4 == 0 and (hh, grp - 2) in flush:
                            ph.pe.wait(ph.sc, flush[(hh, grp - 2)])
                        nb = (psC if grp % 2 == 0 else psU)
                        tp = (0, 64) if hh else None
                        tmm[gi] = ph.pe.op("matmul", nb[b:b + 1, (qt % 4) * 128:(qt % 4 + 1) * 128],
                                           negc[:, co:co + 1], ident[:],
                                           start=True, stop=True, tile_position=tp)
                        if qt % 4 == 3:
                            ph.sc.wait(ph.pe, tmm[gi])
                            flush[(hh, grp)] = ph.sc.op("activation",
                                                        negm[b:b + 1, grp * 512:(grp + 1) * 512],
                                                        nb[b:b + 1, :], AF.Copy)
                            negflush = flush[(hh, grp)]

                    gi = 0
                    pend = None
                    for hh in range(2):
                        b = 64 * hh
                        for qt in range(32):
                            nch = qt // 4 + 1
                            for kc in range(nch):
                                q_, n_ = psq_hist[kc % 2]
                                ph.pe.wait(q_, n_)
                                mm = ph.pe.op("matmul", psQ[kc % 2][:, :],
                                              q2[b:b + 64, qt * 128:(qt + 1) * 128],
                                              k2[b:b + 64, kc * 512:(kc + 1) * 512],
                                              start=True, stop=True)
                                ph.ve.wait(ph.pe, mm)
                                if kc == 0:
                                    ph.ve.wait(ph.ve)
                                n = ph.ve.op("reduce_max", mx8[:, (gi % 2) * 8 + kc:(gi % 2) * 8 + kc + 1],
                                             psQ[kc % 2][:, :], axis=AX.X)
                                psq_hist[kc % 2] = ("ve", n)
                            if pend is not None:
                                mtail(*pend)
                            pend = (hh, qt, nch, gi)
                            gi += 1
                    mtail(*pend)
                    ev_prev = [0]
                    att_prev = 0
                    up_prev = 0
                    pss_hist = {0: ("sc", geo_done), 1: ("sc", geo_done)}
                    pv_last = 0
                    for hh in range(2):
                        b = 64 * hh
                        for qg in range(NTG):
                            sl = slice(qg * TG, (qg + 1) * TG)
                            kcmax = 4 * qg + 4
                            expd = {}
                            pvmm = {}

                            def emit_pv(kc, kcmax=kcmax, hh=hh):
                                ph.pe.wait(ph.sc, expd[kc])
                                if kc == 0 and ev_prev[0]:
                                    ph.pe.wait(ph.sc, ev_prev[0])
                                pvmm[kc] = ph.pe.op("matmul", psB[0:65, :], v2v[:, kc, hh, :],
                                                    ptile[kc % 3][:], start=(kc == 0),
                                                    stop=(kc == kcmax - 1))

                            for kc in range(kcmax):
                                if (hh, qg) == (0, 0) and kc == 0:
                                    ph.pe.wait(ph.sc, negflush)
                                q_, n_ = pss_hist[kc % 2]
                                ph.pe.wait(q_, n_)
                                ph.pe.raw("matmul", psS[kc % 2][:, :],
                                          k2[b:b + 64, kc * 128:(kc + 1) * 128], q2[b:b + 64, sl],
                                          start=True, stop=False)
                                ph.pe.raw("matmul", psS[kc % 2][:, :],
                                          kgeo[b:b + 1, kc * 128:(kc + 1) * 128], qgeo[b:b + 1, sl],
                                          start=False, stop=False)
                                stmm = ph.pe.op("matmul", psS[kc % 2][:, :],
                                                ones[b:b + 1, 0:128], negm[b:b + 1, sl],
                                                start=False, stop=True)
                                diag = kc >= kcmax - 4
                                if diag:
                                    ph.ve.wait(ph.pe, stmm)
                                    mk = ph.ve.op("tensor_tensor", psS[kc % 2][:, :], psS[kc % 2][:, :],
                                                  cv(ctx, "CAUSAL", kc - 4 * qg), ALU.add)
                                    ph.sc.wait(ph.ve, mk)
                                else:
                                    ph.sc.wait(ph.pe, stmm)
                                if kc >= 3:
                                    ph.sc.wait(ph.pe, pvmm[kc - 3])
                                elif pv_last:
                                    ph.sc.wait(ph.pe, pv_last)
                                expd[kc] = ph.sc.op("activation", ptile[kc % 3][:], psS[kc % 2][:, :], AF.Exp)
                                pss_hist[kc % 2] = ("sc", expd[kc])
                                if kc >= 1:
                                    emit_pv(kc - 1)
                            emit_pv(kcmax - 1)
                            pv_last = pvmm[kcmax - 1]
                            ph.sc.wait(ph.pe, pv_last)
                            d1 = ph.sc.op("activation", dent[64:65, :], psB[64:65, :], AF.Copy)
                            ph.ve.wait(ph.sc, d1)
                            d2 = ph.ve.op("reciprocal", dent[64:65, :], dent[64:65, :])
                            ph.pe.wait(ph.ve, d2)
                            if att_prev:
                                ph.pe.wait(ph.ve, att_prev)
                            bcmm = ph.pe.op("matmul", psC[0:64, :], onesf[64:65, 0:64],
                                            dent[64:65, :], start=True, stop=True)
                            ev_prev[0] = ph.sc.op("activation", anorm[:], psB[0:64, :], AF.Copy)
                            ph.ve.wait(ph.sc, ev_prev[0])
                            ph.ve.wait(ph.pe, bcmm)
                            if hh == 0:
                                att_prev = ph.ve.op("tensor_tensor", ATTN[pair][0:64, sl],
                                                    anorm[:], psC[0:64, :], ALU.mult)
                            else:
                                if up_prev:
                                    ph.ve.wait(ph.sc, up_prev)
                                att_prev = ph.ve.op("tensor_tensor", utmp[:], anorm[:],
                                                    psC[0:64, :], ALU.mult)
                                ph.pe.wait(ph.ve, att_prev)
                                umm = ph.pe.op("matmul", psU[64:128, :], ident[0:64, 0:64],
                                               utmp[:], start=True, stop=True, tile_position=(0, 64))
                                ph.sc.wait(ph.pe, umm)
                                up_prev = ph.sc.op("activation", ATTN[pair][64:128, sl],
                                                   psU[64:128, :], AF.Copy)
                ctx.phase(f"l3a_p{pair}", pairbody)

        wos = ctx.sb("l3wos", [128, 8 * 1024], BF16, stack=es_out)
        wosv = wos[:].rearrange("p (n f) -> p n f", n=NC)

        def wo_load(ph):
            ph.sw.dma(wosv, wo_d.rearrange("(n p) f -> p n f", p=128))
            ph.sw.wait(ph.sw)
        ctx.phase("l3a_wol", wo_load)

        psW = [ctx.ps(f"l3W{i}", [128, TG], F32, stack=es_out) for i in range(2)]
        stt_hist = {}

        def produce(ph, tg, vsl, marks, prev):
            for c in range(NC):
                if c >= 2:
                    ph.pe.wait(ph.ve, stt_hist[(tg, c - 2)])
                elif tg > 0:
                    ph.pe.wait(ph.ve, stt_hist[(tg - 1, c + 6)])
                for ic in range(NC - 1):
                    ph.pe.raw("matmul", psW[c % 2][:, :], wosv[:, ic, c * 128:(c + 1) * 128],
                              ATTN[ic][:, tg * TG:(tg + 1) * TG], start=(ic == 0), stop=False)
                mmd = ph.pe.op("matmul", psW[c % 2][:, :], wosv[:, NC - 1, c * 128:(c + 1) * 128],
                               ATTN[NC - 1][:, tg * TG:(tg + 1) * TG], start=False, stop=True)
                ph.ve.wait(ph.pe, mmd)
                if prev is not None:
                    ph.ve.wait(ph.ve, prev["app"][c])
                    ph.ve.wait(ph.pe, prev["summ"])
                    ph.ve.wait(ph.sc, prev["sq"][c])
                stt_hist[(tg, c)] = ph.ve.op("scalar_tensor_tensor", vsl[c][:], psW[c % 2][:, :], 1.0,
                                             X[c][:, tg * TG:(tg + 1) * TG], ALU.mult, ALU.add)
                marks[(tg, c)] = ("ve", stt_hist[(tg, c)])

        ln_finish(ctx, "l3a_ln", produce)



def emit_l1_attn(ctx):
    X = [ctx.t[f"X{c}"] for c in range(NC)]
    wq_d, wk_d, wv_d, wo_d = (ctx.d[k] for k in ["l1a_wq", "l1a_wk", "l1a_wv", "l1a_wo"])
    bq_d, bk_d, bv_d, bo_d = (ctx.d[k] for k in ["l1a_bq", "l1a_bk", "l1a_bv", "l1a_bo"])
    lnq_d, lnk_d = ctx.d["l1a_lnq"], ctx.d["l1a_lnk"]
    es_out = contextlib.ExitStack()
    with es_out:
        ATTN = [ctx.sb(f"l1attn{c}", [128, T], BF16, stack=es_out) for c in range(NC)]
        bo_s = ctx.sb("l1a_bos", [128, 8], F32, stack=es_out)
        es = contextlib.ExitStack()
        with es:
            q2 = ctx.sb("l1q2", [128, T], BF16, stack=es)
            k2 = ctx.sb("l1k2", [128, T], BF16, stack=es)
            qrawf = ctx.sb("l1qraw", [128, 2 * TG], F32, stack=es)
            sqf = ctx.sb("l1sqf", [128, TG], F32, stack=es)
            tapp = ctx.sb("l1tapp", [128, TG], F32, stack=es)
            lnm = ctx.sb("l1lnm", [2, TG], F32, stack=es)
            lnv = ctx.sb("l1lnv", [2, TG], F32, stack=es)
            lnr = ctx.sb("l1lnr", [2, TG], F32, stack=es)
            lnmr = ctx.sb("l1lnmr", [2, TG], F32, stack=es)
            gb = ctx.sb("l1gb", [128, 8], F32, stack=es)
            bqk_s = ctx.sb("l1bqk", [128, 2 * 8], F32, stack=es)
            bv_s = ctx.sb("l1bv", [128, 8], F32, stack=es)
            v2 = ctx.sb("l1v2", [128, 32 * 2 * 65], BF16, stack=es)
            negm = ctx.sb("l1negm", [128, T], BF16, stack=es)
            mx8 = ctx.sb("l1mx8", [128, 8], F32, stack=es)
            negc = ctx.sb("l1negc", [128, 1], BF16, stack=es)
            ptile = [ctx.sb(f"l1pt{i}", [128, TG], BF16, stack=es) for i in range(3)]
            dent = ctx.sb("l1dent", [128, TG], F32, stack=es)
            anorm = ctx.sb("l1an", [64, TG], F32, stack=es)
            utmp = ctx.sb("l1ut", [64, TG], BF16, stack=es)
            wqs = ctx.sb("l1wqs", [128, 8 * 128], BF16, stack=es)
            wks = ctx.sb("l1wks", [128, 8 * 128], BF16, stack=es)
            wvs = ctx.sb("l1wvs", [128, 8 * 128], BF16, stack=es)
            psQ = [ctx.ps(f"l1Q{i}", [128, TG], F32, stack=es) for i in range(2)]
            psS = [ctx.ps(f"l1S{i}", [128, TG], F32, stack=es) for i in range(2)]
            psB = ctx.ps("l1B", [128, TG], F32, stack=es)
            psC = ctx.ps("l1C", [128, TG], F32, stack=es)
            psU = ctx.ps("l1U", [128, TG], F32, stack=es)
            v2v = v2[:].rearrange("p (tc hs f) -> p tc hs f", tc=32, hs=2)
            qrawv = qrawf[:].rearrange("p (i f) -> p i f", i=2)
            wqsv = wqs[:].rearrange("p (n f) -> p n f", n=NC)
            wksv = wks[:].rearrange("p (n f) -> p n f", n=NC)
            wvsv = wvs[:].rearrange("p (n f) -> p n f", n=NC)

            def setup(ph):
                ph.hw.dma(gb[:, 0:2], lnq_d)
                ph.hw.dma(gb[:, 2:4], lnk_d)
                ph.hw.dma(bqk_s[:, 0:8], bq_d)
                ph.hw.dma(bqk_s[:, 8:16], bk_d)
                ph.hw.dma(bv_s[:], bv_d)
                ph.hw.dma(bo_s[:], bo_d)
                ph.ve.wait(ph.hw)
                n = ph.ve.op("tensor_scalar_mul", gb[:, 0:2], gb[:, 0:2], 0.125)
                ph.ve.wait(ph.ve, n)
            ctx.phase("l1a_setup", setup)

            for pair in range(8):
                def pairbody(ph, pair=pair):
                    ident = ctx.t["s_IDENT"]
                    ones = ctx.t["s_ONES"]
                    onesf = ctx.t["s_ONESF"]
                    pat2 = ctx.t["s_PAT2F"]
                    pat2t = ctx.t["s_PAT2TF"]
                    csl = slice(pair * 128, (pair + 1) * 128)
                    ph.sw.dma(wvsv, wv_d[:, csl].rearrange("(n p) f -> p n f", p=128))
                    swv = ph.sw.n
                    ph.sw.dma(wqsv, wq_d[:, csl].rearrange("(n p) f -> p n f", p=128))
                    swq = ph.sw.n
                    ph.sw.dma(wksv, wk_d[:, csl].rearrange("(n p) f -> p n f", p=128))
                    swk = ph.sw.n
                    mset = ph.ve.op("memset", v2v[:, :, :, 64:65], 1.0)
                    evac = {}
                    for tc in range(32):
                        if tc == 0:
                            ph.pe.wait(ph.sw, swv)
                        if tc >= 2:
                            ph.pe.wait(ph.sc, evac[tc - 2])
                        for c in range(NC - 1):
                            ph.pe.raw("matmul", psQ[tc % 2][:, 0:128],
                                      X[c][:, tc * 128:(tc + 1) * 128], wvsv[:, c, :],
                                      start=(c == 0), stop=False)
                        mmd = ph.pe.op("matmul", psQ[tc % 2][:, 0:128],
                                       X[NC - 1][:, tc * 128:(tc + 1) * 128], wvsv[:, NC - 1, :],
                                       start=False, stop=True)
                        ph.sc.wait(ph.pe, mmd)
                        if tc == 0:
                            ph.sc.wait(ph.ve, mset)
                        for hs in range(2):
                            evac[tc] = ph.sc.op("activation", v2v[:, tc, hs, 0:64],
                                                psQ[tc % 2][:, hs * 64:(hs + 1) * 64], AF.Copy)
                    vproj_done = evac[31]
                    apply_done = {}
                    evd = {}
                    sqmm = {}
                    stats_done = {}
                    bc_done = {}
                    for qk in range(2):
                        dst = q2 if qk == 0 else k2
                        wsv = wqsv if qk == 0 else wksv
                        bcol = bqk_s[:, pair:pair + 1] if qk == 0 else bqk_s[:, 8 + pair:9 + pair]
                        gcol = gb[:, 0:1] if qk == 0 else gb[:, 2:3]
                        bgcol = gb[:, 1:2] if qk == 0 else gb[:, 3:4]
                        for tg in range(NTG):
                            i = qk * 8 + tg
                            sl = slice(tg * TG, (tg + 1) * TG)
                            if i == 0:
                                ph.pe.wait(ph.sw, swq)
                                ph.pe.wait(ph.sc, vproj_done)
                            if i == 8:
                                ph.pe.wait(ph.sw, swk)
                            if i >= 2:
                                ph.pe.wait(ph.sc, evd[i - 2])
                            for c in range(NC - 1):
                                ph.pe.raw("matmul", psQ[i % 2][:, :], wsv[:, c, :],
                                          X[c][:, sl], start=(c == 0), stop=False)
                            pmm = ph.pe.op("matmul", psQ[i % 2][:, :], wsv[:, NC - 1, :],
                                           X[NC - 1][:, sl], start=False, stop=True)
                            ph.sc.wait(ph.pe, pmm)
                            if i >= 2:
                                ph.sc.wait(ph.ve, apply_done[i - 2])
                            evd[i] = ph.sc.op("activation", qrawv[:, i % 2, :], psQ[i % 2][:, :],
                                              AF.Identity, bias=bcol)
                            if i >= 1:
                                ph.sc.wait(ph.pe, sqmm[i - 1])
                            sq_ = ph.sc.op("activation", sqf[:], qrawv[:, i % 2, :], AF.Square)
                            if i >= 1:
                                ph.pe.wait(ph.ve, stats_done[i - 1])
                            ph.pe.wait(ph.sc, evd[i])
                            ph.pe.op("matmul", psC[0:2, :], pat2[:], qrawv[:, i % 2, :],
                                     start=True, stop=True)
                            ph.pe.wait(ph.sc, sq_)
                            sqmm[i] = ph.pe.op("matmul", psU[0:2, :], pat2[:], sqf[:],
                                               start=True, stop=True)
                            ph.ve.wait(ph.pe, sqmm[i])
                            if i >= 1:
                                ph.ve.wait(ph.ve, apply_done[i - 1])
                            ph.ve.op("tensor_scalar_mul", lnm[:], psC[0:2, :], 1.0 / 64)
                            n = ph.ve.op("tensor_scalar_mul", lnv[:], psU[0:2, :], 1.0 / 64)
                            ph.ve.wait(ph.ve, n)
                            n = ph.ve.op("tensor_tensor", lnr[:], lnm[:], lnm[:], ALU.mult)
                            ph.ve.wait(ph.ve, n)
                            n = ph.ve.op("tensor_tensor", lnv[:], lnv[:], lnr[:], ALU.subtract)
                            ph.ve.wait(ph.ve, n)
                            stats_done[i] = ph.ve.op("tensor_scalar_add", lnv[:], lnv[:], LNEPS)
                            ph.sc.wait(ph.ve, stats_done[i])
                            sq2 = ph.sc.op("activation", lnr[:], lnv[:], AF.Sqrt)
                            ph.ve.wait(ph.sc, sq2)
                            n = ph.ve.op("reciprocal", lnr[:], lnr[:])
                            ph.ve.wait(ph.ve, n)
                            mr_ = ph.ve.op("tensor_tensor", lnmr[:], lnm[:], lnr[:], ALU.mult)
                            ph.pe.wait(ph.ve, mr_)
                            if i >= 1:
                                ph.pe.wait(ph.ve, apply_done[i - 1])
                            ph.pe.op("matmul", psS[0][:, :], pat2t[:], lnr[:], start=True, stop=True)
                            bc_done[i] = ph.pe.op("matmul", psS[1][:, :], pat2t[:], lnmr[:],
                                                  start=True, stop=True)
                            ph.ve.wait(ph.pe, bc_done[i])
                            n = ph.ve.op("tensor_tensor", tapp[:], qrawv[:, i % 2, :], psS[0][:, :], ALU.mult)
                            ph.ve.wait(ph.ve, n)
                            n = ph.ve.op("tensor_tensor", tapp[:], tapp[:], psS[1][:, :], ALU.subtract)
                            apply_done[i] = n
                            ph.sc.wait(ph.ve, n)
                            fin = ph.sc.op("activation", dst[:, sl], tapp[:], AF.Identity,
                                           scale=gcol, bias=bgcol)
                            apply_done[i] = max(apply_done[i], 0)
                            evd[i] = max(evd[i], fin * 0)
                            if True:
                                evd[i] = evd[i]
                            stats_done[i] = stats_done[i]
                            apply_fin = fin
                    qk_ready_sc = apply_fin
                    tmm_prev = 0
                    negflush = 0
                    psq_hist = {0: ("sc", qk_ready_sc), 1: ("sc", qk_ready_sc)}
                    first_max = True
                    for hh in range(2):
                        b = 64 * hh
                        for qt in range(32):
                            if first_max:
                                ph.pe.wait(ph.ve, stats_done[15])
                                first_max = False
                            kclo = max(0, (qt * 128 - 256) // 512)
                            kchi = qt // 4
                            red = {}
                            nch = kchi - kclo + 1
                            for kci, kc in enumerate(range(kclo, kchi + 1)):
                                q_, n_ = psq_hist[kci % 2]
                                ph.pe.wait(q_, n_)
                                mm = ph.pe.op("matmul", psQ[kci % 2][:, :],
                                              q2[b:b + 64, qt * 128:(qt + 1) * 128],
                                              k2[b:b + 64, kc * 512:(kc + 1) * 512],
                                              start=True, stop=True)
                                ph.ve.wait(ph.pe, mm)
                                red[kci] = ph.ve.op("reduce_max", mx8[:, kci:kci + 1],
                                                    psQ[kci % 2][:, :], axis=AX.X)
                                psq_hist[kci % 2] = ("ve", red[kci])
                            ph.ve.wait(ph.ve, red[nch - 1])
                            if tmm_prev:
                                ph.ve.wait(ph.pe, tmm_prev)
                            negv = ph.ve.op("reduce_max", negc[:, 0:1], mx8[:, 0:nch],
                                            axis=AX.X, negate=True)
                            ph.pe.wait(ph.ve, negv)
                            if qt % 4 == 0 and negflush:
                                ph.pe.wait(ph.sc, negflush)
                            nb = (psC if (qt // 4) % 2 == 0 else psU)
                            tp = (0, 64) if hh else None
                            tmm_prev = ph.pe.op("matmul", nb[b:b + 1, (qt % 4) * 128:(qt % 4 + 1) * 128],
                                                negc[:, 0:1], ident[:],
                                                start=True, stop=True, tile_position=tp)
                            if qt % 4 == 3:
                                ph.sc.wait(ph.pe, tmm_prev)
                                negflush = ph.sc.op("activation",
                                                    negm[b:b + 1, (qt // 4) * 512:(qt // 4 + 1) * 512],
                                                    nb[b:b + 1, :], AF.Copy)
                    ev_prev = [0]
                    att_prev = 0
                    up_prev = 0
                    pss_hist = {0: ("pe", bc_done[15]), 1: ("pe", bc_done[15])}
                    pv_last = 0
                    for hh in range(2):
                        b = 64 * hh
                        for qg in range(NTG):
                            sl = slice(qg * TG, (qg + 1) * TG)
                            kclo = max(0, 4 * qg - 2)
                            kcmax = 4 * qg + 4
                            expd = {}
                            pvmm = {}
                            ncc = kcmax - kclo

                            def emit_pv(kci, ncc=ncc, hh=hh, kclo=kclo):
                                ph.pe.wait(ph.sc, expd[kci])
                                if kci == 0 and ev_prev[0]:
                                    ph.pe.wait(ph.sc, ev_prev[0])
                                pvmm[kci] = ph.pe.op("matmul", psB[0:65, :],
                                                     v2v[:, kclo + kci, hh, :],
                                                     ptile[kci % 3][:], start=(kci == 0),
                                                     stop=(kci == ncc - 1))

                            for kci, kc in enumerate(range(kclo, kcmax)):
                                if (hh, qg) == (0, 0) and kci == 0:
                                    ph.pe.wait(ph.sc, negflush)
                                    ph.pe.wait(ph.ve, apply_done[15])
                                q_, n_ = pss_hist[kci % 2]
                                ph.pe.wait(q_, n_)
                                ph.pe.raw("matmul", psS[kci % 2][:, :],
                                          k2[b:b + 64, kc * 128:(kc + 1) * 128], q2[b:b + 64, sl],
                                          start=True, stop=False)
                                stmm = ph.pe.op("matmul", psS[kci % 2][:, :],
                                                ones[b:b + 1, 0:128], negm[b:b + 1, sl],
                                                start=False, stop=True)
                                widx = (qg * 512 - kc * 128 + 384) // 128
                                ph.ve.wait(ph.pe, stmm)
                                mk = ph.ve.op("tensor_tensor", psS[kci % 2][:, :], psS[kci % 2][:, :],
                                              cv(ctx, "WINM", widx), ALU.add)
                                ph.sc.wait(ph.ve, mk)
                                if kci >= 3:
                                    ph.sc.wait(ph.pe, pvmm[kci - 3])
                                elif pv_last:
                                    ph.sc.wait(ph.pe, pv_last)
                                expd[kci] = ph.sc.op("activation", ptile[kci % 3][:], psS[kci % 2][:, :], AF.Exp)
                                pss_hist[kci % 2] = ("sc", expd[kci])
                                if kci >= 1:
                                    emit_pv(kci - 1)
                            emit_pv(ncc - 1)
                            pv_last = pvmm[ncc - 1]
                            ph.sc.wait(ph.pe, pv_last)
                            d1 = ph.sc.op("activation", dent[64:65, :], psB[64:65, :], AF.Copy)
                            ph.ve.wait(ph.sc, d1)
                            d2 = ph.ve.op("reciprocal", dent[64:65, :], dent[64:65, :])
                            ph.pe.wait(ph.ve, d2)
                            if att_prev:
                                ph.pe.wait(ph.ve, att_prev)
                            bcmm = ph.pe.op("matmul", psC[0:64, :], onesf[64:65, 0:64],
                                            dent[64:65, :], start=True, stop=True)
                            ev_prev[0] = ph.sc.op("activation", anorm[:], psB[0:64, :], AF.Copy)
                            ph.ve.wait(ph.sc, ev_prev[0])
                            ph.ve.wait(ph.pe, bcmm)
                            if hh == 0:
                                att_prev = ph.ve.op("tensor_tensor", utmp[:], anorm[:],
                                                    psC[0:64, :], ALU.mult)
                                ph.sc.wait(ph.ve, att_prev)
                                up_prev = ph.sc.op("activation", ATTN[pair][0:64, sl], utmp[:],
                                                   AF.Identity, bias=bv_s[0:64, pair:pair + 1])
                            else:
                                if up_prev:
                                    ph.ve.wait(ph.sc, up_prev)
                                att_prev = ph.ve.op("tensor_tensor", utmp[:], anorm[:],
                                                    psC[0:64, :], ALU.mult)
                                ph.pe.wait(ph.ve, att_prev)
                                umm = ph.pe.op("matmul", psU[64:128, :], ident[0:64, 0:64],
                                               utmp[:], start=True, stop=True, tile_position=(0, 64))
                                ph.sc.wait(ph.pe, umm)
                                up_prev = ph.sc.op("activation", ATTN[pair][64:128, sl],
                                                   psU[64:128, :], AF.Identity, bias=bv_s[64:128, pair:pair + 1])
                ctx.phase(f"l1a_p{pair}", pairbody)

        wos = ctx.sb("l1wos", [128, 8 * 1024], BF16, stack=es_out)
        wosv = wos[:].rearrange("p (n f) -> p n f", n=NC)

        def wo_load(ph):
            ph.sw.dma(wosv, wo_d.rearrange("(n p) f -> p n f", p=128))
            ph.sw.wait(ph.sw)
        ctx.phase("l1a_wol", wo_load)

        psW = [ctx.ps(f"l1W{i}", [128, TG], F32, stack=es_out) for i in range(2)]
        stt_hist = {}

        def produce(ph, tg, vsl, marks, prev):
            for c in range(NC):
                if c >= 2:
                    ph.pe.wait(ph.ve, stt_hist[(tg, c - 2)])
                elif tg > 0:
                    ph.pe.wait(ph.ve, stt_hist[(tg - 1, c + 6)])
                for ic in range(NC - 1):
                    ph.pe.raw("matmul", psW[c % 2][:, :], wosv[:, ic, c * 128:(c + 1) * 128],
                              ATTN[ic][:, tg * TG:(tg + 1) * TG], start=(ic == 0), stop=False)
                mmd = ph.pe.op("matmul", psW[c % 2][:, :], wosv[:, NC - 1, c * 128:(c + 1) * 128],
                               ATTN[NC - 1][:, tg * TG:(tg + 1) * TG], start=False, stop=True)
                ph.ve.wait(ph.pe, mmd)
                if prev is not None:
                    ph.ve.wait(ph.ve, prev["app"][c])
                    ph.ve.wait(ph.pe, prev["summ"])
                    ph.ve.wait(ph.sc, prev["sq"][c])
                stt_hist[(tg, c)] = ph.ve.op("scalar_tensor_tensor", vsl[c][:], psW[c % 2][:, :],
                                             bo_s[:, c:c + 1], X[c][:, tg * TG:(tg + 1) * TG],
                                             ALU.add, ALU.add)
                marks[(tg, c)] = ("ve", stt_hist[(tg, c)])

        ln_finish(ctx, "l1a_ln", produce)




def tmax_scalar(ph, ctx, col_ap, psum1, out11, negate=False):
    identf = ctx.t["s_IDENTF"]
    mm = ph.pe.op("matmul", psum1[0:1, 0:128], col_ap, identf[:], start=True, stop=True)
    ph.ve.wait(ph.pe, mm)
    return ph.ve.op("reduce_max", out11, psum1[0:1, 0:128], axis=AX.X, negate=negate)


def tsum_scalar(ph, ctx, col_ap, psum1, out11):
    onesf = ctx.t["s_ONESF"]
    mm = ph.pe.op("matmul", psum1[0:1, 133:134], onesf[:, 0:1], col_ap, start=True, stop=True)
    ph.sc.wait(ph.pe, mm)
    return ph.sc.op("activation", out11, psum1[0:1, 133:134], AF.Copy)


def bcast128(ph, ctx, row_ap, psumout):
    onesf = ctx.t["s_ONESF"]
    return ph.pe.op("matmul", psumout, onesf[0:1, :], row_ap, start=True, stop=True)


def excl_cumsum_tokens(ph, ctx, maskb_ap, out_ap, psA, psB, row32, col32):
    ltriu = ctx.t["s_LTRIU"]
    ones = ctx.t["s_ONES"]
    ident = ctx.t["s_IDENT"]
    ph.pe.wait(ph.ve)
    ph.pe.wait(ph.sc)
    mm1 = ph.pe.op("matmul", psA[0][:, 0:32], ltriu[:], maskb_ap, start=True, stop=True)
    cs = ph.pe.op("matmul", psA[1][0:1, 0:32], ones[:, 0:1], maskb_ap, start=True, stop=True)
    ph.ve.wait(ph.pe, cs)
    n = ph.ve.op("tensor_copy", row32[0:1, 0:32], psA[1][0:1, 0:32])
    ph.pe.wait(ph.ve, n)
    mmT = ph.pe.op("matmul", psA[1][0:32, 32:40], row32[0:1, 0:32], ones[0:1, 0:8], start=True, stop=True)
    ph.ve.wait(ph.pe, mmT)
    n = ph.ve.op("tensor_copy", col32[:, 0:2], psA[1][0:32, 32:34])
    ph.pe.wait(ph.ve, n)
    mm2 = ph.pe.op("matmul", psA[1][0:32, 64:66], ltriu[0:32, 0:32], col32[:, 0:2], start=True, stop=True)
    ph.ve.wait(ph.pe, mm2)
    n = ph.ve.op("tensor_copy", ctx.t["s_f2cc"][0:32, 0:1], psA[1][0:32, 64:65])
    ph.ve.wait(ph.ve, n)
    n = ph.ve.op("tensor_scalar_mul", ctx.t["s_f2cc"][0:32, 1:2], ctx.t["s_f2cc"][0:32, 0:1], 0.125)
    ph.ve.wait(ph.ve, n)
    n = ph.ve.op("tensor_copy", ctx.t["s_f2cci"][0:32, 0:1], ctx.t["s_f2cc"][0:32, 1:2])
    ph.ve.wait(ph.ve, n)
    n = ph.ve.op("tensor_copy", ctx.t["s_f2cc"][0:32, 2:3], ctx.t["s_f2cci"][0:32, 0:1])
    ph.ve.wait(ph.ve, n)
    n = ph.ve.op("tensor_tensor", ctx.t["s_f2cc"][0:32, 3:4], ctx.t["s_f2cc"][0:32, 2:3],
                 ctx.t["s_f2cc"][0:32, 1:2], ALU.is_gt)
    ph.ve.wait(ph.ve, n)
    n = ph.ve.op("tensor_tensor", ctx.t["s_f2cc"][0:32, 2:3], ctx.t["s_f2cc"][0:32, 2:3],
                 ctx.t["s_f2cc"][0:32, 3:4], ALU.subtract)
    ph.ve.wait(ph.ve, n)
    n = ph.ve.op("tensor_scalar_mul", ctx.t["s_f2cc"][0:32, 2:3], ctx.t["s_f2cc"][0:32, 2:3], 8.0)
    ph.ve.wait(ph.ve, n)
    n = ph.ve.op("tensor_tensor", ctx.t["s_f2cc"][0:32, 3:4], ctx.t["s_f2cc"][0:32, 0:1],
                 ctx.t["s_f2cc"][0:32, 2:3], ALU.subtract)
    ph.ve.wait(ph.ve, n)
    n = ph.ve.op("tensor_copy", col32[:, 2:3], ctx.t["s_f2cc"][0:32, 2:3])
    n = ph.ve.op("tensor_copy", col32[:, 3:4], ctx.t["s_f2cc"][0:32, 3:4])
    ph.pe.wait(ph.ve, n)
    mh = ph.pe.op("matmul", psA[1][0:1, 96:128], col32[:, 2:3], ident[0:32, 0:32], start=True, stop=True)
    ml = ph.pe.op("matmul", psA[1][0:1, 128:160], col32[:, 3:4], ident[0:32, 0:32], start=True, stop=True)
    ph.ve.wait(ph.pe, ml)
    n = ph.ve.op("tensor_copy", row32[0:1, 0:32], psA[1][0:1, 96:128])
    n = ph.ve.op("tensor_copy", row32[0:1, 32:64], psA[1][0:1, 128:160])
    ph.pe.wait(ph.ve, n)
    ph.pe.op("matmul", psB[:, 0:32], ones[0:1, :], row32[0:1, 0:32], start=True, stop=True)
    bc = ph.pe.op("matmul", psB[:, 64:96], ones[0:1, :], row32[0:1, 32:64], start=True, stop=True)
    ph.sc.wait(ph.pe, bc)
    ev = ph.sc.op("activation", out_ap, psA[0][:, 0:32], AF.Copy)
    ph.ve.wait(ph.sc, ev)
    n = ph.ve.op("tensor_tensor", out_ap, out_ap, psB[:, 0:32], ALU.add)
    ph.ve.wait(ph.ve, n)
    n = ph.ve.op("tensor_tensor", out_ap, out_ap, psB[:, 64:96], ALU.add)
    return n


def transpose_tile(ph, ctx, fm_view, tok_off, xtm_tile, psA, guard):
    ident = ctx.t["s_IDENT"]
    evs = []
    for half in range(2):
        if guard.get(half):
            q, n = guard[half]
            ph.pe.wait(q, n)
        for cc in range(4):
            c = half * 4 + cc
            if cc < 3:
                ph.pe.raw("matmul", psA[half][:, cc * 128:(cc + 1) * 128],
                          fm_view(c), ident[:], start=True, stop=True)
            else:
                mm = ph.pe.op("matmul", psA[half][:, cc * 128:(cc + 1) * 128],
                              fm_view(c), ident[:], start=True, stop=True)
        ph.sc.wait(ph.pe, mm)
        ev = ph.sc.op("activation", xtm_tile[:, half * 512:(half + 1) * 512],
                      psA[half][:, :], AF.Copy)
        guard[half] = (ph.sc, ev)
        evs.append(ev)
    return evs[-1]



def emit_l2_ffn(ctx):
    X = [ctx.t[f"X{c}"] for c in range(NC)]
    d = ctx.d
    G_d = ctx.dram_tmp("l2f_G", [1024, 1024], BF16)
    Gm_d = ctx.dram_tmp("l2f_Gm", [1024, 16], F32)
    Z_d = ctx.dram_tmp("l2f_Z", [T, 1024], BF16)
    es = contextlib.ExitStack()
    with es:
        s128 = ctx.sb("f2_s", [128, 32], F32, stack=es)
        brow = ctx.sb("f2_b", [128, 32], F32, stack=es)
        urow = ctx.sb("f2_u", [128, 32], F32, stack=es)
        erow = ctx.sb("f2_e", [128, 32], F32, stack=es)
        acol = ctx.sb("f2_a", [128, 1], F32, stack=es)
        mcol = ctx.sb("f2_m", [128, 1], F32, stack=es)
        s11 = ctx.sb("f2_s11", [1, 8], F32, stack=es)
        lo_c = ctx.sb("f2_lo", [128, 1], F32, stack=es)
        hi_c = ctx.sb("f2_hi", [128, 1], F32, stack=es)
        mid_c = ctx.sb("f2_mid", [128, 1], F32, stack=es)
        sel11 = ctx.sb("f2_sel", [1, 1], F32, stack=es)
        need_c = ctx.sb("f2_need", [128, 1], F32, stack=es)
        ind = ctx.sb("f2_ind", [128, 32], F32, stack=es)
        indb = ctx.sb("f2_indb", [128, 32], BF16, stack=es)
        above = ctx.sb("f2_abv", [128, 32], F32, stack=es)
        mask = ctx.sb("f2_msk", [128, 32], F32, stack=es)
        maskb = ctx.sb("f2_mskb", [128, 32], BF16, stack=es)
        ec = ctx.sb("f2_ec", [128, 32], F32, stack=es)
        slotf = ctx.sb("f2_slot", [128, 32], F32, stack=es)
        idxf = ctx.sb("f2_idxf", [128, 32], F32, stack=es)
        idxi = ctx.sb("f2_idxi", [128, 32], I32, stack=es)
        row32 = ctx.sb("f2_r32", [1, 64], BF16, stack=es)
        col32 = ctx.sb("f2_c32", [32, 4], BF16, stack=es)
        ctx.sb("s_f2cs", [1, 64], F32, stack=es)
        ctx.sb("s_f2csi", [1, 64], I32, stack=es)
        ctx.sb("s_f2cc", [32, 4], F32, stack=es)
        ctx.sb("s_f2cci", [32, 1], I32, stack=es)
        rts = ctx.sb("f2_rts", [128, 8], BF16, stack=es)
        es_r = contextlib.ExitStack()
        ctx.sb("f2_abb", [1, T], BF16, stack=es)
        ctx.sb("f2_tib", [1, T], BF16, stack=es)
        ctx.sb("f2_srow", [1, T], F32, stack=es_r)
        ctx.sb("f2_browr", [1, T], F32, stack=es_r)
        ctx.sb("f2_urowr", [1, T], F32, stack=es_r)
        erowr = ctx.sb("f2_erowr", [1, T], F32, stack=es_r)
        psB = ctx.ps("f2B", [128, TG], F32, stack=es)
        es2 = contextlib.ExitStack()
        psA = [ctx.ps(f"f2A{i}", [128, TG], F32, stack=es2) for i in range(2)]

        def routing(ph):
            onesf = ctx.t["s_ONESF"]
            ident = ctx.t["s_IDENT"]
            srow_r = ctx.t["f2_srow"]
            browr = ctx.t["f2_browr"]
            urowr = ctx.t["f2_urowr"]
            ph.sw.dma(rts[:], d["l2f_rt"])
            ph.pe.wait(ph.sw)
            ev = 0
            for tg in range(NTG):
                if ev:
                    ph.pe.wait(ph.sc, ev)
                for c in range(NC - 1):
                    ph.pe.raw("matmul", psA[tg % 2][0:1, :], rts[:, c:c + 1],
                              X[c][:, tg * TG:(tg + 1) * TG], start=(c == 0), stop=False)
                mm = ph.pe.op("matmul", psA[tg % 2][0:1, :], rts[:, NC - 1:NC],
                              X[NC - 1][:, tg * TG:(tg + 1) * TG], start=False, stop=True)
                ph.sc.wait(ph.pe, mm)
                ev = ph.sc.op("activation", srow_r[0:1, tg * TG:(tg + 1) * TG],
                              psA[tg % 2][0:1, :], AF.Copy)
            ph.ve.wait(ph.sc, ev)
            ph.ve.op("memset", s11[:], 0.0)
            n = ph.ve.op("tensor_scalar_mul", browr[:], srow_r[:], -1.0)
            ph.ve.wait(ph.ve, n)
            logk = float(np.log(np.float32(1152.0)))
            for it in range(L2F_RITER if L2F_RSUB >= 2 else 0):
                n = ph.ve.op("tensor_tensor", urowr[:], srow_r[:], browr[:], ALU.add)
                ph.ve.wait(ph.ve, n)
                n = ph.ve.op("reduce_max", s11[:, 0:1], urowr[:], axis=AX.X, negate=True)
                ph.sc.wait(ph.ve, n)
                if L2F_RBODY < 3:
                    continue
                e = ph.sc.op("activation", erowr[:], urowr[:], AF.Exp, bias=s11[0:1, 0:1])
                ph.ve.wait(ph.sc, e)
                if L2F_RBODY < 4:
                    continue
                n = ph.ve.op("reduce_sum", s11[:, 3:4], erowr[:], axis=AX.X)
                ph.sc.wait(ph.ve, n)
                lnv = ph.sc.op("activation", s11[:, 1:2], s11[:, 3:4], AF.Ln)
                ph.ve.wait(ph.sc, lnv)
                if L2F_RBODY < 5:
                    continue
                n = ph.ve.op("tensor_tensor", s11[:, 2:3], s11[:, 1:2], s11[:, 0:1], ALU.subtract)
                ph.ve.wait(ph.ve, n)
                n = ph.ve.op("tensor_scalar", s11[:, 2:3], s11[:, 2:3], -1.0, logk, ALU.mult, ALU.add)
                ph.ve.wait(ph.ve, n)
                n = ph.ve.op("tensor_scalar", urowr[:], srow_r[:], s11[0:1, 2:3], None, ALU.add)
                ph.ve.wait(ph.ve, n)
                n = ph.ve.op("tensor_scalar_max", browr[:], urowr[:], 0.0)
                ph.ve.wait(ph.ve, n)
                n = ph.ve.op("tensor_scalar_mul", browr[:], browr[:], -1.0)
                ph.ve.wait(ph.ve, n)
            n = ph.ve.op("tensor_tensor", urowr[:], urowr[:], browr[:], ALU.add)
            ph.sc.wait(ph.ve, n)
            scf = ph.sc.op("activation", srow_r[:], urowr[:], AF.Exp)
            ph.ve.wait(ph.sc, scf)
            if L2F_RSUB < 3:
                return
            ph.ve.op("memset", s11[:, 4:5], 0.0)
            n = ph.ve.op("memset", s11[:, 5:6], 1.01)
            ph.ve.wait(ph.ve, n)
            for it in range(42):
                n = ph.ve.op("tensor_tensor", s11[:, 6:7], s11[:, 4:5], s11[:, 5:6], ALU.add)
                ph.ve.wait(ph.ve, n)
                n = ph.ve.op("tensor_scalar_mul", s11[:, 6:7], s11[:, 6:7], 0.5)
                ph.ve.wait(ph.ve, n)
                n = ph.ve.op("tensor_single_scalar", urowr[:], srow_r[:], s11[0:1, 6:7], ALU.is_ge)
                ph.ve.wait(ph.ve, n)
                n = ph.ve.op("reduce_sum", s11[:, 7:8], urowr[:], axis=AX.X)
                ph.ve.wait(ph.ve, n)
                n = ph.ve.op("tensor_single_scalar", s11[:, 3:4], s11[:, 7:8], 1024.0, ALU.is_ge)
                ph.ve.wait(ph.ve, n)
                n = ph.ve.op("tensor_tensor", s11[:, 0:1], s11[:, 6:7], s11[:, 4:5], ALU.subtract)
                ph.ve.wait(ph.ve, n)
                n = ph.ve.op("tensor_tensor", s11[:, 0:1], s11[:, 0:1], s11[:, 3:4], ALU.mult)
                ph.ve.wait(ph.ve, n)
                n = ph.ve.op("tensor_tensor", s11[:, 4:5], s11[:, 4:5], s11[:, 0:1], ALU.add)
                n = ph.ve.op("tensor_tensor", s11[:, 1:2], s11[:, 5:6], s11[:, 6:7], ALU.subtract)
                ph.ve.wait(ph.ve, n)
                n = ph.ve.op("tensor_tensor", s11[:, 2:3], s11[:, 1:2], s11[:, 3:4], ALU.mult)
                ph.ve.wait(ph.ve, n)
                n = ph.ve.op("tensor_tensor", s11[:, 1:2], s11[:, 1:2], s11[:, 2:3], ALU.subtract)
                ph.ve.wait(ph.ve, n)
                n = ph.ve.op("tensor_tensor", s11[:, 5:6], s11[:, 5:6], s11[:, 1:2], ALU.subtract)
                ph.ve.wait(ph.ve, n)
            if L2F_RSUB < 4:
                return
            n = ph.ve.op("tensor_single_scalar", urowr[:], srow_r[:], s11[0:1, 5:6], ALU.is_ge)
            ph.ve.wait(ph.ve, n)
            n = ph.ve.op("tensor_single_scalar", browr[:], srow_r[:], s11[0:1, 4:5], ALU.is_ge)
            ph.ve.wait(ph.ve, n)
            n = ph.ve.op("tensor_tensor", browr[:], browr[:], urowr[:], ALU.subtract)
            ph.ve.wait(ph.ve, n)
            n = ph.ve.op("reduce_sum", s11[:, 3:4], urowr[:], axis=AX.X)
            ph.ve.wait(ph.ve, n)
            need_n = ph.ve.op("tensor_scalar", s11[:, 3:4], s11[:, 3:4], -1.0, 1024.0, ALU.mult, ALU.add)
            n = ph.ve.op("tensor_copy", ctx.t["f2_abb"][:], urowr[:])
            n = ph.ve.op("tensor_copy", ctx.t["f2_tib"][:], browr[:])
            ph.pe.wait(ph.ve, n)
            ones_b = ctx.t["s_ONES"]
            for nm, rsrct, dst in (("ab", ctx.t["f2_abb"], above), ("ti", ctx.t["f2_tib"], ind)):
                rsrc = rsrct[:]
                evl = 0
                prevv = {0: 0, 1: 0}
                for tc in range(32):
                    if prevv[tc % 2]:
                        ph.pe.wait(ph.sc, prevv[tc % 2])
                    mm = ph.pe.op("matmul", psA[tc % 2][:, 256:272],
                                  rsrc[0:1, tc * 128:(tc + 1) * 128], ones_b[0:1, 0:16],
                                  start=True, stop=True)
                    ph.sc.wait(ph.pe, mm)
                    evl = ph.sc.op("activation", dst[:, tc:tc + 1], psA[tc % 2][:, 256:257], AF.Copy)
                    prevv[tc % 2] = evl
                ph.ve.wait(ph.sc, evl)
            bc = bcast128(ph, ctx, s11[0:1, 0:8], psB[:, 40:48])
            ph.ve.wait(ph.pe, bc)
            n = ph.ve.op("tensor_copy", need_c[:], psB[:, 43:44])
            ph.ve.wait(ph.ve, n)
            if L2F_RSUB < 5:
                return
            n = ph.ve.op("tensor_copy", indb[:], ind[:])
            ph.ve.wait(ph.ve, n)
            n = excl_cumsum_tokens(ph, ctx, indb[:], ec[:], psA, psB, row32, col32)
            ph.ve.wait(ph.ve, n)
            if L2F_RSUB < 6:
                return
            n = ph.ve.op("tensor_single_scalar", urow[:], ec[:], need_c[:, 0:1], ALU.is_lt)
            ph.ve.wait(ph.ve, n)
            n = ph.ve.op("tensor_tensor", urow[:], urow[:], ind[:], ALU.mult)
            ph.ve.wait(ph.ve, n)
            n = ph.ve.op("tensor_tensor", mask[:], above[:], urow[:], ALU.add)
            ph.ve.wait(ph.ve, n)
            n = ph.ve.op("tensor_copy", maskb[:], mask[:])
            ph.ve.wait(ph.ve, n)
            if L2F_RSUB < 7:
                return
            n = excl_cumsum_tokens(ph, ctx, maskb[:], slotf[:], psA, psB, row32, col32)
            ph.ve.wait(ph.ve, n)
            if L2F_RSUB < 8:
                return
            n = ph.ve.op("tensor_scalar", idxf[:], mask[:], -1.0, 1.0, ALU.mult, ALU.add)
            ph.ve.wait(ph.ve, n)
            n = ph.ve.op("tensor_scalar_mul", idxf[:], idxf[:], 1.0e6)
            ph.ve.wait(ph.ve, n)
            n = ph.ve.op("tensor_tensor", urow[:], slotf[:], mask[:], ALU.mult)
            ph.ve.wait(ph.ve, n)
            n = ph.ve.op("tensor_tensor", idxf[:], idxf[:], urow[:], ALU.add)
            ph.ve.wait(ph.ve, n)
            n = ph.ve.op("tensor_copy", idxi[:], idxf[:])
            ph.ve.wait(ph.ve, n)
        ctx.phase("l2f_route", routing)
        es_r.close()
        sqt = ctx.sb("f2_sqt", [128, TG], F32, stack=es)
        rn = ctx.sb("f2_rn", [1, T], F32, stack=es)
        xhat = ctx.sb("f2_xh", [128, 8 * TG], BF16, stack=es)
        xtm = ctx.sb("f2_xtm", [128, 1024], BF16, stack=es)
        xhv = xhat[:].rearrange("p (n f) -> p n f", n=NC)
        if L2F_STAGES < 2:
            es2.close()
            return

        def scatter_x(ph):
            onesf = ctx.t["s_ONESF"]
            tok16 = ctx.t["s_TOKID16"]
            zz = ph.ve.op("memset", xtm[:], 0.0)
            ph.hw.wait(ph.ve, zz)
            for tt in range(32):
                ph.hw.dma(Z_d[tt * 128:(tt + 1) * 128, :], xtm[:])
            zdone = ph.hw.n
            guard = {}
            scat_prev = 0
            scat_hist = {}
            trlast = 0
            for tg in range(NTG):
                sl = slice(tg * TG, (tg + 1) * TG)
                sqmm = 0
                for c in range(NC):
                    if c == 0 and tg > 0:
                        ph.sc.wait(ph.ve, xh_done)
                    if sqmm:
                        ph.sc.wait(ph.pe, sqmm)
                    sq = ph.sc.op("activation", sqt[:], X[c][:, sl], AF.Square)
                    ph.pe.wait(ph.sc, sq)
                    sqmm = ph.pe.op("matmul", psB[0:1, :], onesf[:, 0:1], sqt[:],
                                    start=(c == 0), stop=(c == NC - 1))
                ph.sc.wait(ph.pe, sqmm)
                nrm = ph.sc.op("activation", rn[0:1, sl], psB[0:1, :], AF.Sqrt)
                ph.ve.wait(ph.sc, nrm)
                n = ph.ve.op("tensor_scalar_max", rn[0:1, sl], rn[0:1, sl], 1e-12)
                ph.ve.wait(ph.ve, n)
                n = ph.ve.op("reciprocal", rn[0:1, sl], rn[0:1, sl])
                ph.pe.wait(ph.ve, n)
                rb = ph.pe.op("matmul", psB[:, :], onesf[0:1, :], rn[0:1, sl], start=True, stop=True)
                ph.ve.wait(ph.pe, rb)
                for c in range(NC):
                    if trlast:
                        ph.ve.wait(ph.pe, trlast)
                    xh_done = ph.ve.op("tensor_tensor", xhv[:, c, :], X[c][:, sl], psB[:, :], ALU.mult)
                for ti in range(4):
                    tt = tg * 4 + ti
                    ph.pe.wait(ph.ve, xh_done)
                    if scat_prev:
                        ph.sc.wait(ph.sw, scat_prev)
                    ev = transpose_tile(ph, ctx, lambda c, ti=ti: xhv[:, c, ti * 128:(ti + 1) * 128],
                                        tt * 128, xtm, psA, guard)
                    trlast = guard[1][1] if False else ev
                    trlast = ph.pe.n
                    ph.sw.wait(ph.sc, ev)
                    if tg == 0 and ti == 0:
                        ph.sw.wait(ph.hw, zdone)
                    ph.sw.op("indirect_dma_start", tick=16,
                             out=G_d, out_offset=IndirectOffsetOnAxis(idxi[:, tt:tt + 1], 0),
                             in_=xtm[:], in_offset=None, bounds_check=ph.sw.reg(1023),
                             oob_is_err=False)
                    scat_prev = ph.sw.op("indirect_dma_start", tick=16,
                                         out=Gm_d, out_offset=IndirectOffsetOnAxis(idxi[:, tt:tt + 1], 0),
                                         in_=tok16[:, tt * 16:(tt + 1) * 16], in_offset=None,
                                         bounds_check=ph.sw.reg(1023), oob_is_err=False)
            ph.gp.wait(ph.sw)
        ctx.phase("l2f_scat", scatter_x)
        if DEBUG_L2F:
            dm = ctx.dram_out("dbg_mask", [128, 32], F32)
            dab = ctx.dram_out("dbg_abv", [128, 32], F32)
            dti = ctx.dram_out("dbg_tie", [128, 32], F32)
            ds11 = ctx.dram_out("dbg_s11", [1, 8], F32)
            dec = ctx.dram_out("dbg_ec", [128, 32], F32)
            dnc = ctx.dram_out("dbg_need", [128, 1], F32)
            dsl = ctx.dram_out("dbg_slot", [128, 32], F32)
            dix = ctx.dram_out("dbg_idx", [128, 32], I32)
            dgm = ctx.dram_out("dbg_gm", [128, 16], F32)
            dg = ctx.dram_out("dbg_g", [128, 1024], BF16)
            dgall = ctx.dram_out("dbg_gall", [1024, 1024], BF16)
            dgmall = ctx.dram_out("dbg_gmall", [1024, 16], F32)

            def dump(ph):
                for q in range(2):
                    ph.hw.dma(dgall[q * 512:(q + 1) * 512, :], G_d[q * 512:(q + 1) * 512, :])
                ph.hw.dma(dgmall, Gm_d)
                ph.hw.dma(dab, above[:])
                ph.hw.dma(dti, ind[:])
                ph.hw.dma(ds11, s11[:])
                ph.hw.dma(dec, ec[:])
                ph.hw.dma(dnc, need_c[:])
                ph.hw.dma(dm, mask[:])
                ph.hw.dma(dsl, slotf[:])
                ph.hw.dma(dix, idxi[:])
                ph.hw.dma(dgm, Gm_d[0:128, :])
                ph.hw.dma(dg, G_d[0:128, :])
                ph.hw.wait(ph.hw)
            ctx.phase("l2f_dbg", dump)
        if L2F_STAGES < 3:
            es2.close()
            return

        es3 = contextlib.ExitStack()
        gh = ctx.sb("f2_gh", [128, 8 * TG], BF16, stack=es3)
        hbuf = ctx.sb("f2_hb", [128, 32 * TG], BF16, stack=es3)
        yfm = ctx.sb("f2_yfm", [128, 8 * TG], BF16, stack=es3)
        gm = ctx.sb("f2_gm", [128, 4], F32, stack=es3)
        gmi = ctx.sb("f2_gmi", [128, 4], I32, stack=es3)
        w1s = [ctx.sb(f"f2_w1s{i}", [128, 8 * 128], BF16, stack=es3) for i in range(2)]
        w2s = [ctx.sb(f"f2_w2s{i}", [128, 32 * 128], BF16, stack=es3) for i in range(2)]
        bh1 = ctx.sb("f2_bh1", [128, 32], F32, stack=es3)
        bh2 = ctx.sb("f2_bh2", [128, 8], F32, stack=es3)
        ghg = ctx.sb("f2_ghg", [128, 8], F32, stack=es3)
        ghv = gh[:].rearrange("p (n f) -> p n f", n=NC)
        hbv = hbuf[:].rearrange("p (n f) -> p n f", n=32)
        yfv = yfm[:].rearrange("p (n f) -> p n f", n=NC)

        def heavy(ph):
            ident = ctx.t["s_IDENT"]
            ph.hw.dma(bh1[:], d["l2f_b1h"])
            ph.hw.dma(bh2[:], d["l2f_b2h"])
            ph.hw.dma(ghg[:], d["l2f_gh"])
            hwc0 = ph.hw.n
            ph.ve.wait(ph.hw, hwc0)
            ghs = ph.ve.op("tensor_scalar_mul", ghg[:], ghg[:], 32.0)
            w1mm = {}
            w2mm = {}
            for sg in range(2):
                if sg > 0:
                    ph.hw.wait(ph.sw)
                    ph.hw.wait(ph.pe)
                gtiles = []
                for si in range(4):
                    r0 = sg * 512 + si * 128
                    ph.hw.dma(xtm[:], G_d[r0:r0 + 128, :]) if False else None
                    gtiles.append(ph.hw.dma(ctx.t["f2_gt%d" % si][:], G_d[r0:r0 + 128, :]))
                    ph.hw.dma(gm[:, si:si + 1], Gm_d[r0:r0 + 128, 0:1],
                              allow_slow_non_contiguous=True)
                hwn = ph.hw.n
                ph.ve.wait(ph.hw, hwn)
                if sg > 0:
                    ph.ve.wait(ph.sw)
                gmid = ph.ve.op("tensor_copy", gmi[:], gm[:])
                guard = {}
                ph.pe.wait(ph.sc)
                ph.pe.wait(ph.hw, hwn)
                for si in range(4):
                    gt = ctx.t["f2_gt%d" % si]
                    for half in range(2):
                        if guard.get(half):
                            q, n = guard[half]
                            ph.pe.wait(q, n)
                        for cc in range(4):
                            c = half * 4 + cc
                            if cc < 3:
                                ph.pe.raw("matmul", psA[half][:, cc * 128:(cc + 1) * 128],
                                          gt[:, c * 128:(c + 1) * 128],
                                          ident[:], start=True, stop=True)
                            else:
                                mm = ph.pe.op("matmul", psA[half][:, cc * 128:(cc + 1) * 128],
                                              gt[:, c * 128:(c + 1) * 128], ident[:],
                                              start=True, stop=True)
                        ph.sc.wait(ph.pe, mm)
                        ph.sc.wait(ph.ve, ghs)
                        for cc in range(4):
                            c = half * 4 + cc
                            ev = ph.sc.op("activation", ghv[:, c, si * 128:(si + 1) * 128],
                                          psA[half][:, cc * 128:(cc + 1) * 128],
                                          AF.Identity, scale=ghg[:, c:c + 1])
                        guard[half] = (ph.sc, ev)
                gh_done = guard[1][1]
                wld = {}
                for hc in range(32):
                    if hc >= 2:
                        ph.sw.wait(ph.pe, w1mm[hc - 2])
                    elif sg > 0:
                        ph.sw.wait(ph.pe, w1mm[30 + hc])
                    ph.sw.dma(w1s[hc % 2][:].rearrange("p (n f) -> p n f", n=NC),
                              d["l2f_w1h"][:, hc * 128:(hc + 1) * 128].rearrange("(n p) f -> p n f", p=128))
                    wld[hc] = ph.sw.n
                    ph.pe.wait(ph.sw, wld[hc])
                    if hc == 0:
                        ph.pe.wait(ph.sc, gh_done)
                    if hc >= 2:
                        ph.pe.wait(ph.sc, hev[hc - 2])
                    w1v = w1s[hc % 2][:].rearrange("p (n f) -> p n f", n=NC)
                    for c in range(NC - 1):
                        ph.pe.raw("matmul", psA[hc % 2][:, :], w1v[:, c, :], ghv[:, c, :],
                                  start=(c == 0), stop=False)
                    mm = ph.pe.op("matmul", psA[hc % 2][:, :], w1v[:, NC - 1, :], ghv[:, NC - 1, :],
                                  start=False, stop=True)
                    w1mm[hc] = mm
                    ph.sc.wait(ph.pe, mm)
                    hev[hc] = ph.sc.op("activation", hbv[:, hc, :], psA[hc % 2][:, :],
                                       AF.Gelu, bias=bh1[:, hc:hc + 1])
                yev = {}
                for oc in range(NC):
                    if oc >= 2:
                        ph.sw.wait(ph.pe, w2mm[oc - 2])
                    elif sg > 0:
                        ph.sw.wait(ph.pe, w2mm[6 + oc])
                    ph.sw.dma(w2s[oc % 2][:].rearrange("p (n f) -> p n f", n=32),
                              d["l2f_w2h"][:, oc * 128:(oc + 1) * 128].rearrange("(n p) f -> p n f", p=128))
                    ph.pe.wait(ph.sw)
                    if oc == 0:
                        ph.pe.wait(ph.sc, hev[31])
                    if oc >= 2:
                        ph.pe.wait(ph.sc, yev[oc - 2])
                    w2v = w2s[oc % 2][:].rearrange("p (n f) -> p n f", n=32)
                    for hc in range(31):
                        ph.pe.raw("matmul", psA[oc % 2][:, :], w2v[:, hc, :], hbv[:, hc, :],
                                  start=(hc == 0), stop=False)
                    mm = ph.pe.op("matmul", psA[oc % 2][:, :], w2v[:, 31, :], hbv[:, 31, :],
                                  start=False, stop=True)
                    w2mm[oc] = mm
                    ph.sc.wait(ph.pe, mm)
                    yev[oc] = ph.sc.op("activation", yfv[:, oc, :], psA[oc % 2][:, :],
                                       AF.Identity, bias=bh2[:, oc:oc + 1])
                guard2 = {}
                zprev = 0
                for si in range(4):
                    ph.pe.wait(ph.sc, yev[7] if 7 in yev else 0)
                    if zprev:
                        ph.sc.wait(ph.sw, zprev)
                    ev = transpose_tile(ph, ctx, lambda c, si=si: yfv[:, c, si * 128:(si + 1) * 128],
                                        0, xtm, psA, guard2)
                    ph.sw.wait(ph.sc, ev)
                    ph.sw.wait(ph.ve, gmid)
                    zprev = ph.sw.op("indirect_dma_start", tick=16,
                                     out=Z_d, out_offset=IndirectOffsetOnAxis(gmi[:, si:si + 1], 0),
                                     in_=xtm[:], in_offset=None, bounds_check=ph.sw.reg(4095),
                                     oob_is_err=False)
            ph.gp.wait(ph.sw)

        hev = {}
        for si in range(4):
            ctx.sb("f2_gt%d" % si, [128, 1024], BF16, stack=es3)
        ctx.phase("l2f_heavy", heavy)
        if DEBUG_L2F:
            dz = ctx.dram_out("dbg_z", [128, 1024], BF16)
            dzall = ctx.dram_out("dbg_zall", [T, 1024], BF16)
            dgmi = ctx.dram_out("dbg_gmi", [128, 4], I32)
            dgh = ctx.dram_out("dbg_gh", [128, 8 * TG], BF16)
            dhb = ctx.dram_out("dbg_hb", [128, 4 * TG], BF16)
            dyf = ctx.dram_out("dbg_yf", [128, 8 * TG], BF16)

            def dumpz(ph):
                for q in range(8):
                    ph.hw.dma(dzall[q * TG:(q + 1) * TG, :], Z_d[q * TG:(q + 1) * TG, :])
                ph.hw.dma(dgmi, gmi[:])
                ph.hw.dma(dz, Z_d[0:128, :])
                ph.hw.dma(dgh, gh[:])
                ph.hw.dma(dhb, hbuf[:, 0:4 * TG])
                ph.hw.dma(dyf, yfm[:])
                ph.hw.wait(ph.hw)
            ctx.phase("l2f_dbgz", dumpz)
        es3.close()
        es2.close()
        if L2F_STAGES < 4:
            return

        w1l = ctx.sb("f2_w1l", [128, 8 * 512], BF16, stack=es)
        w2l = ctx.sb("f2_w2l", [128, 4 * 1024], BF16, stack=es)
        b1l = ctx.sb("f2_b1l", [128, 4], F32, stack=es)
        b2l = ctx.sb("f2_b2l", [128, 8], F32, stack=es)
        glg = ctx.sb("f2_gl", [128, 8], F32, stack=es)
        hl = ctx.sb("f2_hl", [128, 4 * TG], BF16, stack=es)
        xng = ctx.sb("f2_xng", [128, 8 * TG], BF16, stack=es)
        ztl = [ctx.sb(f"f2_ztl{i}", [128, 1024], BF16, stack=es) for i in range(4)]
        psW = [ctx.ps(f"f2W{i}", [128, TG], F32, stack=es) for i in range(2)]
        w1lv = w1l[:].rearrange("p (n f) -> p n f", n=NC)
        w2lv = w2l[:].rearrange("p (n f) -> p n f", n=4)
        hlv = hl[:].rearrange("p (n f) -> p n f", n=4)
        xngv = xng[:].rearrange("p (n f) -> p n f", n=NC)

        def lw_load(ph):
            ph.sw.dma(w1lv, d["l2f_w1l"].rearrange("(n p) f -> p n f", p=128))
            ph.sw.dma(w2lv, d["l2f_w2l"].rearrange("(n p) f -> p n f", p=128))
            ph.hw.dma(b1l[:], d["l2f_b1l"])
            ph.hw.dma(b2l[:], d["l2f_b2l"])
            ph.hw.dma(glg[:], d["l2f_gl"])
            ph.ve.wait(ph.hw)
            ph.ve.op("tensor_scalar_mul", glg[:], glg[:], 32.0)
            ph.gp.wait(ph.sw)
        ctx.phase("l2f_lwl", lw_load)

        stt_hist = {}

        def produce(ph, tg, vsl, marks, prev):
            onesf = ctx.t["s_ONESF"]
            ident = ctx.t["s_IDENT"]
            sl = slice(tg * TG, (tg + 1) * TG)
            rb = ph.pe.op("matmul", psW[0][:, :], onesf[0:1, :], rn[0:1, sl], start=True, stop=True)
            ph.ve.wait(ph.pe, rb)
            if tg > 0:
                ph.ve.wait(ph.pe, stt_hist[("hmm", tg - 1)])
            xh_done = 0
            for c in range(NC):
                xh_done = ph.ve.op("tensor_tensor", xngv[:, c, :], X[c][:, sl], psW[0][:, :], ALU.mult)
            ph.sc.wait(ph.ve, xh_done)
            gl_done = 0
            for c in range(NC):
                gl_done = ph.sc.op("activation", xngv[:, c, :], xngv[:, c, :],
                                   AF.Identity, scale=glg[:, c:c + 1])
            hev2 = {}
            for hc in range(4):
                if hc == 0:
                    ph.pe.wait(ph.sc, gl_done)
                if hc >= 2:
                    ph.pe.wait(ph.sc, hev2[hc - 2])
                elif tg > 0:
                    ph.pe.wait(ph.sc, stt_hist[("hev", tg - 1)])
                for c in range(NC - 1):
                    ph.pe.raw("matmul", psW[c * 0 + (hc % 2)][:, :], w1lv[:, c, hc * 128:(hc + 1) * 128],
                              xngv[:, c, :], start=(c == 0), stop=False)
                mm = ph.pe.op("matmul", psW[hc % 2][:, :], w1lv[:, NC - 1, hc * 128:(hc + 1) * 128],
                              xngv[:, NC - 1, :], start=False, stop=True)
                stt_hist[("hmm", tg)] = mm
                ph.sc.wait(ph.pe, mm)
                hev2[hc] = ph.sc.op("activation", hlv[:, hc, :], psW[hc % 2][:, :],
                                    AF.Gelu, bias=b1l[:, hc:hc + 1])
            stt_hist[("hev", tg)] = hev2[3]
            zld = 0
            if tg > 0:
                ph.hw.wait(ph.pe, stt_hist[("zmm", tg - 1)])
            for ti in range(4):
                zld = ph.hw.dma(ztl[ti][:], Z_d[(tg * 4 + ti) * 128:(tg * 4 + ti + 1) * 128, :])
            for c in range(NC):
                if c >= 2:
                    ph.pe.wait(ph.ve, stt_hist[(tg, c - 2)])
                elif tg > 0:
                    ph.pe.wait(ph.ve, stt_hist[(tg - 1, c + 6)])
                if c == 0:
                    ph.pe.wait(ph.sc, hev2[3])
                for hc in range(3):
                    ph.pe.raw("matmul", psW[c % 2][:, :], w2lv[:, hc, c * 128:(c + 1) * 128],
                              hlv[:, hc, :], start=(hc == 0), stop=False)
                mm = ph.pe.op("matmul", psW[c % 2][:, :], w2lv[:, 3, c * 128:(c + 1) * 128],
                              hlv[:, 3, :], start=False, stop=True)
                if c == 0:
                    ph.pe.wait(ph.hw, zld)
                if (tg, c) != (0, 0):
                    ph.pe.wait(ph.ve, stt_hist.get(("zread", tg, c - 1), stt_hist.get(("zread", tg - 1, NC - 1), 0)))
                for ti in range(3):
                    ph.pe.raw("matmul", psB[:, ti * 128:(ti + 1) * 128],
                              ztl[ti][:, c * 128:(c + 1) * 128], ident[:], start=True, stop=True)
                zmm = ph.pe.op("matmul", psB[:, 384:512],
                               ztl[3][:, c * 128:(c + 1) * 128], ident[:], start=True, stop=True)
                stt_hist[("zmm", tg)] = zmm
                ph.ve.wait(ph.pe, zmm)
                if prev is not None:
                    ph.ve.wait(ph.ve, prev["app"][c])
                    ph.ve.wait(ph.pe, prev["summ"])
                    ph.ve.wait(ph.sc, prev["sq"][c])
                n = ph.ve.op("scalar_tensor_tensor", vsl[c][:], psW[c % 2][:, :],
                             b2l[:, c:c + 1], X[c][:, sl], ALU.add, ALU.add)
                ph.ve.wait(ph.ve, n)
                stt_hist[(tg, c)] = ph.ve.op("tensor_tensor", vsl[c][:], vsl[c][:], psB[:, :], ALU.add)
                stt_hist[("zread", tg, c)] = stt_hist[(tg, c)]
                marks[(tg, c)] = ("ve", stt_hist[(tg, c)])

        ln_finish(ctx, "l2f_ln", produce)



BIGM = 1.0e30


def emit_moe_ffn(ctx, pre, sigmoid_route, has_bias, base2x, div8, n_shared):
    d = ctx.d
    X = [ctx.t[f"X{c}"] for c in range(NC)]
    es2 = contextlib.ExitStack()
    HC = 32
    PASS = 1024
    w1b_d = ctx.dram_tmp(pre + "_w1b", [8 * 1024, 4096], BF16)
    w2b_d = ctx.dram_tmp(pre + "_w2b", [8 * 4096, 1024], BF16)
    if n_shared:
        w1s_d = ctx.dram_tmp(pre + "_w1sb", [n_shared * 1024, 4096], BF16)
        w2s_d = ctx.dram_tmp(pre + "_w2sb", [n_shared * 4096, 1024], BF16)

    w8row = ctx.sb(pre + "_w8", [8, T], F32, stack=es2)
    wgs = ctx.sb(pre + "_wgs", [128, 8 * 8], BF16, stack=es2)
    if has_bias:
        b2s = ctx.sb(pre + "_b2s", [8, 1024], F32, stack=es2)
    if sigmoid_route:
        rbrow = ctx.sb(pre + "_rb", [128, 8], F32, stack=es2)

    def precast(ph):
        for e in range(8):
            ph.sw.dma(w1b_d[e * 1024:(e + 1) * 1024, :], d[pre + "_w1f"][e * 1024:(e + 1) * 1024, :])
            ph.sw.dma(w2b_d[e * 4096:(e + 1) * 4096, :], d[pre + "_w2f"][e * 4096:(e + 1) * 4096, :])
        for n in range(n_shared):
            ph.sw.dma(w1s_d[n * 1024:(n + 1) * 1024, :], d[pre + "_w1sf"][n * 1024:(n + 1) * 1024, :])
            ph.sw.dma(w2s_d[n * 4096:(n + 1) * 4096, :], d[pre + "_w2sf"][n * 4096:(n + 1) * 4096, :])
        ph.sw.dma(wgs[:].rearrange("p (n f) -> p n f", n=8),
                  d[pre + "_wg"].rearrange("(n p) f -> p n f", p=128))
        if has_bias:
            ph.hw.dma(b2s[:], d[pre + "_b2"])
        if sigmoid_route:
            ph.hw.dma(rbrow[:], d[pre + "_rbf"])
        ph.gp.wait(ph.sw)
        ph.gp.wait(ph.hw)
    ctx.phase(pre + "_cast", precast)

    es_g = contextlib.ExitStack()
    tA = [ctx.sb(pre + f"_tA{i}", [128, 64], F32, stack=es_g) for i in range(2)]
    rw = [ctx.sb(pre + f"_rw{i}", [128, 16], F32, stack=es_g) for i in range(2)]
    pG = [ctx.ps(pre + f"_pG{i}", [128, 8], F32, stack=es_g) for i in range(2)]
    pT = [ctx.ps(pre + f"_pT{i}", [8, 128], F32, stack=es_g) for i in range(2)]

    def gate(ph):
        identf = ctx.t["s_IDENTF"]
        wgv = wgs[:].rearrange("p (n f) -> p n f", n=8)
        vend = {}
        send = {}
        gev = {}
        w8ev = {}
        for tt in range(32):
            b = tt % 2
            A = tA[b]
            R = rw[b]
            g = A[:, 0:8]
            eq = A[:, 8:16]
            gm = A[:, 16:24]
            msk = A[:, 24:32]
            z = A[:, 32:40]
            ez = A[:, 40:48]
            rs = A[:, 48:56]
            gb = A[:, 56:64]
            nm1, nm2, cnt, t2, dm, t3, nkth, nzm, ssum, rr = [R[:, i:i + 1] for i in range(10)]
            sl = slice(tt * 128, (tt + 1) * 128)
            if tt == 0:
                ph.pe.wait(ph.sw)
            if tt >= 2:
                ph.pe.wait(ph.sc, gev[tt - 2])
            for c in range(NC):
                mmg = ph.pe.op("matmul", pG[b][:, :], X[c][:, sl], wgv[:, c, :],
                               start=(c == 0), stop=(c == NC - 1))
            ph.sc.wait(ph.pe, mmg)
            if tt >= 2:
                ph.sc.wait(ph.ve, vend[tt - 2])
            gev[tt] = ph.sc.op("activation", g, pG[b][:, :], AF.Identity)
            if sigmoid_route:
                ph.sc.wait(ph.sc)
                rsn = ph.sc.op("activation", rs, g, AF.Sigmoid)
            ph.ve.wait(ph.sc, gev[tt] if not sigmoid_route else rsn)
            sel = g
            if sigmoid_route:
                ph.ve.op("tensor_tensor", gb, g, rbrow[:], ALU.add)
                ph.ve.wait(ph.ve)
                sel = gb
            n = ph.ve.op("reduce_max", nm1, sel, axis=AX.X, negate=True)
            ph.sc.wait(ph.ve, n)
            gdn = ph.sc.op("activation", gm, sel, AF.Identity, bias=nm1)
            ph.ve.wait(ph.sc, gdn)
            n = ph.ve.op("tensor_single_scalar", eq, gm, 0.0, ALU.is_equal)
            ph.sc.wait(ph.ve, n)
            cntn = ph.sc.op("activation", ez, eq, AF.Identity, accum_out=cnt)
            ph.ve.wait(ph.ve)
            ph.ve.op("scalar_tensor_tensor", gm, eq, -BIGM, sel, ALU.mult, ALU.add)
            ph.ve.wait(ph.ve)
            ph.ve.op("reduce_max", nm2, gm, axis=AX.X, negate=True)
            ph.ve.wait(ph.sc, cntn)
            ph.ve.op("tensor_single_scalar", t2, cnt, 2.0, ALU.is_ge)
            ph.ve.wait(ph.ve)
            ph.ve.op("tensor_tensor", dm, t2, nm1, ALU.mult)
            ph.ve.op("tensor_scalar", t3, t2, -1.0, 1.0, ALU.mult, ALU.add)
            ph.ve.wait(ph.ve)
            ph.ve.op("tensor_tensor", t3, t3, nm2, ALU.mult)
            ph.ve.wait(ph.ve)
            n = ph.ve.op("tensor_tensor", nkth, dm, t3, ALU.add)
            ph.sc.wait(ph.ve, n)
            ph.sc.wait(ph.sc)
            gkn = ph.sc.op("activation", msk, sel, AF.Identity, bias=nkth)
            ph.ve.wait(ph.sc, gkn)
            ph.ve.op("tensor_single_scalar", msk, msk, 0.0, ALU.is_ge)
            ph.ve.wait(ph.ve)
            ph.ve.op("tensor_scalar_sub", msk, msk, 1.0)
            ph.ve.wait(ph.ve)
            val = rs if sigmoid_route else g
            ph.ve.op("scalar_tensor_tensor", z, msk, BIGM, val, ALU.mult, ALU.add)
            ph.ve.wait(ph.ve)
            nzmn = ph.ve.op("reduce_max", nzm, z, axis=AX.X, negate=True)
            ph.sc.wait(ph.ve, nzmn)
            ezn = ph.sc.op("activation", ez, z, AF.Exp, bias=nzm, accum_out=ssum)
            ph.ve.wait(ph.sc, ezn)
            vend[tt] = ph.ve.op("reciprocal", rr, ssum)
            ph.sc.wait(ph.ve, vend[tt])
            wcn = ph.sc.op("activation", z, ez, AF.Identity, scale=rr)
            ph.pe.wait(ph.sc, wcn)
            if tt >= 2:
                ph.pe.wait(ph.sc, w8ev[tt - 2])
            mmt = ph.pe.op("matmul", pT[b][:, :], z, identf[:], start=True, stop=True)
            ph.sc.wait(ph.pe, mmt)
            w8ev[tt] = ph.sc.op("activation", w8row[0:8, sl], pT[b][:, :], AF.Identity,
                                scale=0.125 if div8 else 1.0)
            send[tt] = w8ev[tt]
    ctx.phase(pre + "_gate", gate)
    es_g.close()
    if DEBUG_MOE:
        dw8 = ctx.dram_out("dbg_w8", [8, T], F32)
        dtA = ctx.dram_out("dbg_tA", [128, 128], F32)
        drw = ctx.dram_out("dbg_rw", [128, 32], F32)

        def dumpw8(ph):
            ph.hw.dma(dw8, w8row[:])
            nu = 64 if sigmoid_route else 48
            ph.hw.dma(dtA[:, 0:nu], tA[0][:, 0:nu])
            ph.hw.dma(dtA[:, 64:64 + nu], tA[1][:, 0:nu])
            ph.hw.dma(drw[:, 0:10], rw[0][:, 0:10])
            ph.hw.dma(drw[:, 16:26], rw[1][:, 0:10])
            ph.hw.wait(ph.hw)
        ctx.phase(pre + "_dbgw8", dumpw8)

    pp = [ctx.ps(pre + f"_pp{i}", [128, TG], F32, stack=es2) for i in range(4)]
    ybuf = ctx.sb(pre + "_yb", [128, NC * PASS], BF16, stack=es2)
    ybv = ybuf[:].rearrange("p (n f) -> p n f", n=NC)

    experts = [("s", n) for n in range(n_shared)] + [("r", e) for e in range(8)]

    for p in range(4):
        base = p * PASS
        es3 = contextlib.ExitStack()
        hbuf = ctx.sb(pre + f"_hb{p}", [128, HC * PASS], BF16, stack=es3)
        hbv = hbuf[:].rearrange("p (n f) -> p n f", n=HC)
        w1b = [ctx.sb(pre + f"_w1b{p}{i}", [128, 8 * 128], BF16, stack=es3) for i in range(2)]
        w2b = [ctx.sb(pre + f"_w2b{p}{i}", [128, HC * 128], BF16, stack=es3) for i in range(2)]
        wbs = ctx.sb(pre + f"_wbs{p}", [128, PASS], BF16, stack=es3)
        if has_bias:
            b1s = ctx.sb(pre + f"_b1s{p}", [1, 4096], BF16, stack=es3)
            onesr = ctx.sb(pre + f"_onr{p}", [1, TG], BF16, stack=es3)
        py = [ctx.ps(pre + f"_py{p}{i}", [128, TG], F32, stack=es3) for i in range(4)]

        def xpass(ph, p=p, base=base, hbuf=hbuf, hbv=hbv, w1b=w1b, w2b=w2b, wbs=wbs,
                  py=py, b1s=(b1s if has_bias else None),
                  onesr=(onesr if has_bias else None)):
            onesf = ctx.t["s_ONESF"]
            onesn = ph.ve.op("memset", onesr[0:1, :], 1.0) if has_bias else 0
            lastH = {}
            lastY = {}
            yev0 = {}
            ymm_end = {}
            hcons = {}
            ycons = {}
            wbsev = None
            lastHend = {}
            for ei, (kind, e) in enumerate(experts):
                routed = kind == "r"
                w1src = w1b_d if routed else w1s_d
                w2src = w2b_d if routed else w2s_d
                if routed:
                    wbmm = {}
                    for th in range(2):
                        if (4 + th) in lastY:
                            q, n = lastY[4 + th]
                            ph.pe.wait(getattr(ph, q), n)
                        wbmm[th] = ph.pe.op("matmul", py[th][:, :],
                                            ctx.t["s_SEL8"][0:8, e * 128:(e + 1) * 128],
                                            w8row[0:8, base + th * TG:base + (th + 1) * TG],
                                            start=True, stop=True)
                    for th in range(2):
                        ph.sc.wait(ph.pe, wbmm[th])
                        if wbsev is not None:
                            q, n = wbsev
                            ph.sc.wait(getattr(ph, q), n)
                        n = ph.sc.op("activation", wbs[:, th * TG:(th + 1) * TG], py[th][:, :],
                                     AF.Identity)
                        lastY[4 + th] = ("sc", n)
                    wbs_ready = n
                if has_bias:
                    if ei > 0:
                        ph.sw.wait(ph.pe)
                    ph.sw.wait(ph.sw)
                    ph.sw.dma(b1s[0:1, :], d[pre + "_b1f"][e:e + 1, :])
                hev_first = True
                for hc in range(HC):
                    bb = hc % 2
                    if bb in hcons:
                        ph.sw.wait(ph.pe, hcons[bb])
                    ph.sw.wait(ph.sw)
                    ph.sw.dma(w1b[bb][:].rearrange("p (n f) -> p n f", n=8),
                              w1src[e * 1024:(e + 1) * 1024, hc * 128:(hc + 1) * 128]
                              .rearrange("(n p) f -> p n f", p=128))
                    wld = ph.sw.n
                    for th in range(2):
                        pool = bb * 2 + th
                        ph.pe.wait(ph.sw, wld)
                        if pool in lastH:
                            q, n = lastH[pool]
                            ph.pe.wait(getattr(ph, q), n)
                        w1v = w1b[bb][:].rearrange("p (n f) -> p n f", n=8)
                        if has_bias and hc == 0 and th == 0:
                            ph.pe.wait(ph.ve, onesn)
                            ph.pe.wait(ph.sw)
                        for c in range(NC):
                            mm = ph.pe.op("matmul", pp[pool][:, :], w1v[:, c, :],
                                          X[c][:, base + th * TG:base + (th + 1) * TG],
                                          start=(c == 0),
                                          stop=(c == NC - 1 and not has_bias))
                        if has_bias:
                            mm = ph.pe.op("matmul", pp[pool][:, :],
                                          b1s[0:1, hc * 128:(hc + 1) * 128],
                                          onesr[0:1, :], start=False, stop=True)
                        hcons[bb] = mm
                        hsl = hbv[:, hc, th * TG:(th + 1) * TG]
                        if routed and not sigmoid_route:
                            ph.ve.wait(ph.pe, mm)
                            if hev_first and ei > 0:
                                ph.ve.wait(ph.pe, ymm_end[ei - 1])
                            if hev_first:
                                ph.ve.wait(ph.sc, wbs_ready)
                            ph.ve.op("tensor_single_scalar", hsl, pp[pool][:, :], 0.0, ALU.max)
                            ph.ve.wait(ph.ve)
                            n = ph.ve.op("tensor_tensor", hsl, hsl,
                                         wbs[:, th * TG:(th + 1) * TG], ALU.mult)
                            lastH[pool] = ("ve", n)
                        elif routed:
                            ph.sc.wait(ph.pe, mm)
                            if hev_first and ei > 0:
                                ph.sc.wait(ph.pe, ymm_end[ei - 1])
                            gn = ph.sc.op("activation", hsl, pp[pool][:, :], AF.Gelu)
                            ph.ve.wait(ph.sc, gn)
                            if hev_first:
                                ph.ve.wait(ph.sc, wbs_ready)
                            n = ph.ve.op("tensor_tensor", hsl, hsl,
                                         wbs[:, th * TG:(th + 1) * TG], ALU.mult)
                            lastH[pool] = ("ve", n)
                        else:
                            ph.sc.wait(ph.pe, mm)
                            if hev_first and ei > 0:
                                ph.sc.wait(ph.pe, ymm_end[ei - 1])
                            n = ph.sc.op("activation", hsl, pp[pool][:, :], AF.Gelu)
                            lastH[pool] = ("sc", n)
                        hev_first = False
                lastHend[ei] = ("ve", ph.ve.n) if routed else ("sc", ph.sc.n)
                if routed:
                    wbsev = lastHend[ei]
                for fc in range(NC):
                    bb = fc % 2
                    if bb in ycons:
                        ph.sw.wait(ph.pe, ycons[bb])
                    ph.sw.wait(ph.sw)
                    ph.sw.dma(w2b[bb][:].rearrange("p (n f) -> p n f", n=HC),
                              w2src[e * 4096:(e + 1) * 4096, fc * 128:(fc + 1) * 128]
                              .rearrange("(n p) f -> p n f", p=128))
                    wld = ph.sw.n
                    w2v = w2b[bb][:].rearrange("p (n f) -> p n f", n=HC)
                    for th in range(2):
                        pool = 4 + bb * 2 + th
                        ph.pe.wait(ph.sw, wld)
                        if fc == 0 and th == 0:
                            for qq in {lastH[i][0] for i in lastH}:
                                ph.pe.wait(getattr(ph, qq),
                                           max(lastH[i][1] for i in lastH if lastH[i][0] == qq))
                        if pool in lastY:
                            q, n = lastY[pool]
                            ph.pe.wait(getattr(ph, q), n)
                        for j in range(HC):
                            mm = ph.pe.op("matmul", py[pool - 4][:, :], w2v[:, j, :],
                                          hbv[:, j, th * TG:(th + 1) * TG],
                                          start=(j == 0), stop=(j == HC - 1))
                        ycons[bb] = mm
                        ysl = ybv[:, fc, th * TG:(th + 1) * TG]
                        if ei == 0:
                            ph.sc.wait(ph.pe, mm)
                            n = ph.sc.op("activation", ysl, py[pool - 4][:, :], AF.Identity)
                            lastY[pool] = ("sc", n)
                            yev0[(fc, th)] = n
                        else:
                            ph.ve.wait(ph.pe, mm)
                            if ei == 1:
                                ph.ve.wait(ph.sc, yev0[(fc, th)])
                            ph.ve.wait(ph.ve)
                            n = ph.ve.op("tensor_tensor", ysl, ysl, py[pool - 4][:, :], ALU.add)
                            lastY[pool] = ("ve", n)
                ymm_end[ei] = ph.pe.n
        ctx.phase(pre + f"_x{p}", xpass)
        es3.close()
        if DEBUG_MOE and p == 0:
            dyb = ctx.dram_out("dbg_yb", [128, NC * PASS], BF16)

            def dumpyb(ph):
                ph.hw.dma(dyb, ybuf[:])
                ph.hw.wait(ph.hw)
            ctx.phase(pre + "_dbgyb", dumpyb)

        def produce(ph, tg, vsl, marks, prev, p=p):
            lbase = (tg - 2 * p) * TG
            b2n = {}
            for c in range(NC):
                sl = slice(tg * TG, (tg + 1) * TG)
                if has_bias:
                    if c >= 2:
                        ph.pe.wait(ph.ve, b2n[c - 2])
                    mm = ph.pe.op("matmul", pp[c % 2][:, :], b2s[0:8, c * 128:(c + 1) * 128],
                                  w8row[0:8, sl], start=True, stop=True)
                    ph.ve.wait(ph.pe, mm)
                if prev is not None:
                    ph.ve.wait(ph.ve, prev["app"][c])
                    ph.ve.wait(ph.pe, prev["summ"])
                    ph.ve.wait(ph.sc, prev["sq"][c])
                if base2x:
                    ph.ve.op("tensor_scalar", vsl[c][:], X[c][:, sl], 2.0, 0.0,
                             ALU.mult, ALU.add)
                    ph.ve.wait(ph.ve)
                    n = ph.ve.op("tensor_tensor", vsl[c][:], vsl[c][:],
                                 ybv[:, c, lbase:lbase + TG], ALU.add)
                else:
                    b2n[c] = ph.ve.op("tensor_tensor", vsl[c][:],
                                      ybv[:, c, lbase:lbase + TG], pp[c % 2][:, :], ALU.add)
                    ph.ve.wait(ph.ve)
                    n = ph.ve.op("tensor_tensor", vsl[c][:], vsl[c][:], X[c][:, sl], ALU.add)
                marks[(tg, c)] = ("ve", n)

        ln_finish(ctx, pre + f"_ln{p}", produce, tgs=[2 * p, 2 * p + 1])
    es2.close()




def build_program(blocks, in_f32, out_f32):
    nc = bass.Bass("TRN2", target_bir_lowering=False)
    es = contextlib.ExitStack()
    with es:
        ctx = Ctx(nc, es)
        xin = ctx.dram_in("xin", [D, T], F32 if in_f32 else BF16)
        xout = ctx.dram_out("xout", [D, T], F32 if out_f32 else BF16)
        for c in range(NC):
            ctx.sb(f"X{c}", [128, T], BF16)
        for b in blocks:
            for nm, shape in WEIGHT_SHAPES[b]:
                ctx.dram_in(nm, shape, F32)
        stream_load(ctx, xin, cast=in_f32)
        for i, b in enumerate(blocks):
            last = (i == len(blocks) - 1)
            es_b = contextlib.ExitStack()
            declare_consts(ctx, CONST_SETS[(b,)], suffix=str(i), stack=es_b)
            if b == "l0a":
                emit_l0_attn(ctx)
            elif b == "l1f":
                emit_l1_ffn(ctx)
            elif b == "l2a":
                emit_l2_attn(ctx)
            elif b == "l3a":
                emit_l3_attn(ctx)
            elif b == "l1a":
                emit_l1_attn(ctx)
            elif b == "l2f":
                emit_l2_ffn(ctx)
            elif b == "l0f":
                emit_moe_ffn(ctx, "l0f", sigmoid_route=False, has_bias=True,
                             base2x=False, div8=True, n_shared=0)
            elif b == "l3f":
                emit_moe_ffn(ctx, "l3f", sigmoid_route=True, has_bias=False,
                             base2x=True, div8=False, n_shared=2)
            else:
                raise ValueError(b)
            es_b.close()
        if out_f32:
            _store_f32(ctx, xout)
        else:
            stream_store(ctx, xout)
    return nc


def _store_f32(ctx, xout):
    X = [ctx.t[f"X{c}"] for c in range(NC)]
    es = contextlib.ExitStack()
    with es:
        f32t = [ctx.sb(f"f32o{i}", [128, T // 2], F32, stack=es) for i in range(2)]

        def body(ph):
            prev = {}
            for c in range(NC):
                for half in range(2):
                    i = c * 2 + half
                    if i >= 2:
                        ph.ve.wait(ph.hw, prev[i - 2])
                    n = ph.ve.op("tensor_copy", f32t[i % 2][:],
                                 X[c][:, half * (T // 2):(half + 1) * (T // 2)])
                    ph.hw.wait(ph.ve, n)
                    prev[i] = ph.hw.dma(xout[c * 128:(c + 1) * 128,
                                             half * (T // 2):(half + 1) * (T // 2)],
                                        f32t[i % 2][:])
            ph.hw.wait(ph.hw)
        ctx.phase("storef32", body)


ALL_CONST_NAMES = ["IDENT", "IDENTF", "ONES", "ONESF", "PSWAP", "RCOS", "RSIN", "CAUSAL", "WINM", "LTRI", "PAT2F", "PAT2TF", "LTRIU", "TOKID", "TOKID16", "SEL8"]
CONST_SETS = {("l0a",): ["IDENT", "ONES", "ONESF", "PSWAP", "RCOS", "RSIN", "CAUSAL"],
              ("l1f",): ["IDENT", "IDENTF", "ONES", "ONESF"],
              ("l2a",): ["IDENT", "ONES", "ONESF"],
              ("l3a",): ["IDENT", "ONES", "ONESF", "CAUSAL"],
              ("l1a",): ["IDENT", "ONES", "ONESF", "WINM", "PAT2F", "PAT2TF"],
              ("l2f",): ["IDENT", "IDENTF", "ONES", "ONESF", "LTRIU", "TOKID16"],
              ("l0f",): ["IDENTF", "ONESF", "SEL8"],
              ("l3f",): ["IDENTF", "ONESF", "SEL8"]}

WEIGHT_SHAPES = {
    "l0a": [("l0a_wq", (D, D)), ("l0a_wk", (D, D)), ("l0a_wv", (D, D)), ("l0a_wo", (D, D))],
    "l1f": [("l1f_kernel", (D, D)), ("l1f_a", (D, 16)), ("l1f_b", (16, D)), ("l1f_bias", (128, 8))],
    "l2a": [("l2a_wq", (D, D)), ("l2a_wk", (D, D)), ("l2a_wv", (D, D)), ("l2a_wo", (D, D)), ("l2a_bo", (128, 8))],
    "l3a": [("l3a_wq", (D, D)), ("l3a_wk", (D, D)), ("l3a_wv", (D, D)), ("l3a_wo", (D, D)),
            ("l3a_dirs", (128, 16)), ("l3a_hs", (1, 16))],
    "l2f": [("l2f_rt", (128, 8)), ("l2f_w1l", (D, 512)), ("l2f_b1l", (128, 4)),
            ("l2f_w2l", (512, D)), ("l2f_b2l", (128, 8)), ("l2f_gl", (128, 8)),
            ("l2f_gh", (128, 8)), ("l2f_w1h", (D, 4096)), ("l2f_b1h", (128, 32)),
            ("l2f_w2h", (4096, D)), ("l2f_b2h", (128, 8))],
    "l1a": [("l1a_wq", (D, D)), ("l1a_wk", (D, D)), ("l1a_wv", (D, D)), ("l1a_wo", (D, D)),
            ("l1a_bq", (128, 8)), ("l1a_bk", (128, 8)), ("l1a_bv", (128, 8)), ("l1a_bo", (128, 8)),
            ("l1a_lnq", (128, 2)), ("l1a_lnk", (128, 2))],
    "l0f": [("l0f_wg", (D, 8)), ("l0f_w1f", (8 * D, 4096)), ("l0f_w2f", (8 * 4096, D)),
            ("l0f_b1f", (8, 4096)), ("l0f_b2", (8, D))],
    "l3f": [("l3f_wg", (D, 8)), ("l3f_rbf", (128, 8)), ("l3f_w1f", (8 * D, 4096)),
            ("l3f_w2f", (8 * 4096, D)), ("l3f_w1sf", (2 * D, 4096)), ("l3f_w2sf", (2 * 4096, D))],
}


def host_weights(blocks, inputs):
    w = {}
    f32 = np.float32
    if "l0a" in blocks:
        w["l0a_wq"] = np.ascontiguousarray(np.asarray(inputs["L0_attn_W_Q"], f32).T)
        w["l0a_wk"] = np.ascontiguousarray(np.asarray(inputs["L0_attn_W_K"], f32).T)
        w["l0a_wv"] = np.ascontiguousarray(np.asarray(inputs["L0_attn_W_V"], f32).T)
        w["l0a_wo"] = np.ascontiguousarray(np.asarray(inputs["L0_attn_W_O"], f32).T)
    if "l2f" in blocks:
        w["l2f_rt"] = np.ascontiguousarray(np.asarray(inputs["L2_ffn_routing_token"], f32).reshape(8, 128).T)
        w["l2f_w1l"] = np.ascontiguousarray(np.asarray(inputs["L2_ffn_w1_l"], f32).T)
        w["l2f_b1l"] = np.ascontiguousarray(np.asarray(inputs["L2_ffn_b1_l"], f32).reshape(4, 128).T)
        w["l2f_w2l"] = np.ascontiguousarray(np.asarray(inputs["L2_ffn_w2_l"], f32).T)
        w["l2f_b2l"] = np.ascontiguousarray(np.asarray(inputs["L2_ffn_b2_l"], f32).reshape(8, 128).T)
        w["l2f_gl"] = np.ascontiguousarray(np.asarray(inputs["L2_ffn_gamma_l"], f32).reshape(8, 128).T)
        w["l2f_gh"] = np.ascontiguousarray(np.asarray(inputs["L2_ffn_gamma_h"], f32).reshape(8, 128).T)
        w["l2f_w1h"] = np.ascontiguousarray(np.asarray(inputs["L2_ffn_w1_h"], f32).T)
        w["l2f_b1h"] = np.ascontiguousarray(np.asarray(inputs["L2_ffn_b1_h"], f32).reshape(32, 128).T)
        w["l2f_w2h"] = np.ascontiguousarray(np.asarray(inputs["L2_ffn_w2_h"], f32).T)
        w["l2f_b2h"] = np.ascontiguousarray(np.asarray(inputs["L2_ffn_b2_h"], f32).reshape(8, 128).T)
    if "l1a" in blocks:
        w["l1a_wq"] = np.ascontiguousarray(np.asarray(inputs["L1_attn_Wq"], f32).T)
        w["l1a_wk"] = np.ascontiguousarray(np.asarray(inputs["L1_attn_Wk"], f32).T)
        w["l1a_wv"] = np.ascontiguousarray(np.asarray(inputs["L1_attn_Wv"], f32).T)
        w["l1a_wo"] = np.ascontiguousarray(np.asarray(inputs["L1_attn_Wo"], f32).T)
        for nm, key in [("l1a_bq", "L1_attn_bq"), ("l1a_bk", "L1_attn_bk"),
                        ("l1a_bv", "L1_attn_bv"), ("l1a_bo", "L1_attn_bo")]:
            w[nm] = np.ascontiguousarray(np.asarray(inputs[key], f32).reshape(8, 128).T)
        gq = np.asarray(inputs["L1_attn_ln_q_w"], f32)
        bq2 = np.asarray(inputs["L1_attn_ln_q_b"], f32)
        gk = np.asarray(inputs["L1_attn_ln_k_w"], f32)
        bk2 = np.asarray(inputs["L1_attn_ln_k_b"], f32)
        w["l1a_lnq"] = np.ascontiguousarray(np.stack([np.tile(gq, 2), np.tile(bq2, 2)], axis=1))
        w["l1a_lnk"] = np.ascontiguousarray(np.stack([np.tile(gk, 2), np.tile(bk2, 2)], axis=1))
    if "l3a" in blocks:
        wqkv = np.asarray(inputs["L3_attn_w_qkv"], f32)
        w["l3a_wq"] = np.ascontiguousarray(wqkv[:, 0:1024])
        w["l3a_wk"] = np.ascontiguousarray(wqkv[:, 1024:2048])
        w["l3a_wv"] = np.ascontiguousarray(wqkv[:, 2048:3072])
        w["l3a_wo"] = np.ascontiguousarray(np.asarray(inputs["L3_attn_w_out"], f32))
        hd = np.asarray(inputs["L3_attn_head_dirs"], f32)
        dirs = np.zeros((128, 16), f32)
        for h in range(16):
            dirs[(h % 2) * 64:(h % 2) * 64 + 8, h] = hd[h]
        w["l3a_dirs"] = dirs
        w["l3a_hs"] = np.ascontiguousarray(np.asarray(inputs["L3_attn_head_scales"], f32).reshape(1, 16))
    if "l2a" in blocks:
        w["l2a_wq"] = np.ascontiguousarray(np.asarray(inputs["L2_attn_wq"], f32).T)
        w["l2a_wk"] = np.ascontiguousarray(np.asarray(inputs["L2_attn_wk"], f32).T)
        w["l2a_wv"] = np.ascontiguousarray(np.asarray(inputs["L2_attn_wv"], f32).T)
        w["l2a_wo"] = np.ascontiguousarray(np.asarray(inputs["L2_attn_wo"], f32).T)
        w["l2a_bo"] = np.ascontiguousarray(np.asarray(inputs["L2_attn_bo"], f32).reshape(8, 128).T)
    if "l1f" in blocks:
        w["l1f_kernel"] = np.ascontiguousarray(np.asarray(inputs["L1_ffn_kernel"], f32))
        w["l1f_a"] = np.ascontiguousarray(np.asarray(inputs["L1_ffn_a"], f32))
        w["l1f_b"] = np.ascontiguousarray(np.asarray(inputs["L1_ffn_b"], f32))
        w["l1f_bias"] = np.ascontiguousarray(np.asarray(inputs["L1_ffn_bias"], f32).reshape(8, 128).T)
    if "l0f" in blocks:
        w["l0f_wg"] = np.ascontiguousarray(np.asarray(inputs["L0_ffn_W_gate"], f32))
        w["l0f_w1f"] = np.ascontiguousarray(np.asarray(inputs["L0_ffn_W1"], f32).reshape(8 * D, 4096))
        w["l0f_w2f"] = np.ascontiguousarray(np.asarray(inputs["L0_ffn_W2"], f32).reshape(8 * 4096, D))
        w["l0f_b1f"] = np.ascontiguousarray(np.asarray(inputs["L0_ffn_b1"], f32))
        w["l0f_b2"] = np.ascontiguousarray(np.asarray(inputs["L0_ffn_b2"], f32))
    if "l3f" in blocks:
        w["l3f_wg"] = np.ascontiguousarray(np.asarray(inputs["L3_ffn_centroids"], f32).T)
        w["l3f_rbf"] = np.ascontiguousarray(np.tile(np.asarray(inputs["L3_ffn_routing_bias"], f32), (128, 1)))
        w["l3f_w1f"] = np.ascontiguousarray(
            np.asarray(inputs["L3_ffn_Wfc_r"], f32).transpose(0, 2, 1).reshape(8 * D, 4096))
        w["l3f_w2f"] = np.ascontiguousarray(
            np.asarray(inputs["L3_ffn_Wproj_r"], f32).transpose(0, 2, 1).reshape(8 * 4096, D))
        w["l3f_w1sf"] = np.ascontiguousarray(
            np.asarray(inputs["L3_ffn_Wfc_sh"], f32).transpose(0, 2, 1).reshape(2 * D, 4096))
        w["l3f_w2sf"] = np.ascontiguousarray(
            np.asarray(inputs["L3_ffn_Wproj_sh"], f32).transpose(0, 2, 1).reshape(2 * 4096, D))
    return w


def run_blocks(blocks, x_fm_percore, inputs, in_f32=True, out_f32=False):
    nc = build_program(blocks, in_f32, out_f32)
    w = host_weights(blocks, inputs)
    consts = {("c_" + k): np.ascontiguousarray(CONSTS[k]) for k in _used_const_names(blocks)}
    in_maps = []
    for core in range(NCORE):
        m = {"xin": x_fm_percore[core]}
        m.update(w)
        m.update(consts)
        in_maps.append(m)
    res = run_bass_kernel_spmd(nc, in_maps, core_ids=list(range(NCORE)))
    if DEBUG_L1F:
        return res.results
    return [r["xout"] for r in res.results]


def _used_const_names(blocks):
    names = []
    for b in blocks:
        for n in CONST_SETS[(b,)]:
            if n not in names:
                names.append(n)
    return names


BLOCK_ORDER = ["l0a", "l0f", "l1a", "l1f", "l2a", "l2f", "l3a", "l3f"]


def kernel(**inputs):
    x = np.asarray(inputs["L0_attn_in_features"], np.float32)
    cur = [np.ascontiguousarray(x[c].T) for c in range(NCORE)]
    nc = build_program(BLOCK_ORDER, in_f32=True, out_f32=True)
    w = host_weights(BLOCK_ORDER, inputs)
    consts = {("c_" + k): np.ascontiguousarray(CONSTS[k])
              for k in _used_const_names(BLOCK_ORDER)}
    in_maps = []
    for c in range(NCORE):
        m = {"xin": cur[c]}
        m.update(w)
        m.update(consts)
        in_maps.append(m)
    res = run_bass_kernel_spmd(nc, in_maps, core_ids=list(range(NCORE)))
    return np.stack([np.ascontiguousarray(np.asarray(r["xout"], np.float32).T)
                     for r in res.results])
